# version 11
# baseline (speedup 1.0000x reference)
import numpy as np
from contextlib import ExitStack

import concourse.bass as bass
import concourse.mybir as mybir
from concourse.bass_utils import run_bass_kernel_spmd

F32 = mybir.dt.float32
BF16 = mybir.dt.bfloat16
I32 = mybir.dt.int32
AF = mybir.ActivationFunctionType
ALU = mybir.AluOpType

P = 128
T = 512
NCH = 4
DTl = 16
NS = 16
SPLIT_J = 2
NKT = 32
NIT = 86
NJT = 32
EPS = 1e-5
GP_J = DTl - SPLIT_J

SEMS = [
    "LD", "LDG", "SETS", "SETV",
    "XBF", "XH", "XP", "DTIN", "BCV", "DTM", "SP", "YI",
    "BC", "CSB", "DA", "SC", "HC", "VCH", "R1", "CC", "GB", "YA", "HS", "SETG",
    "CP", "HCB", "XPH", "BCD", "PP", "PPD",
    "LNL", "SQ", "S12", "STV", "STS", "MBC", "ZV", "ZBF", "ZSC",
    "WG", "WU", "MMG", "MMU", "SIL", "GUV", "GUS",
    "GUL", "WD", "MMD", "ZJL", "R2V", "SQ2", "S34", "R2S",
    "ST2V", "ST2S", "M2BC", "R2L", "OV", "OUT",
]


def build():
    nc = bass.Bass("TRN2", target_bir_lowering=False)

    xfm = nc.dram_tensor("xfm", [4096, 2048], F32, kind="ExternalInput").ap()
    xhh = nc.dram_tensor("xhh", [2048, 2048], F32, kind="ExternalInput").ap()
    xpwt = nc.dram_tensor("xpwt", [128, 5120], F32, kind="ExternalInput").ap()
    dtwt = nc.dram_tensor("dtwt", [128, 2048], F32, kind="ExternalInput").ap()
    dtbi = nc.dram_tensor("dtbi", [128, 16], F32, kind="ExternalInput").ap()
    alog = nc.dram_tensor("alog", [128, 256], F32, kind="ExternalInput").ap()
    dvec = nc.dram_tensor("dvec", [128, 16], F32, kind="ExternalInput").ap()
    ones1 = nc.dram_tensor("ones1", [128, 1], F32, kind="ExternalInput").ap()
    onebc = nc.dram_tensor("onebc", [1, 128], F32, kind="ExternalInput").ap()
    wg = nc.dram_tensor("wg", [NIT, 128, 4096], F32, kind="ExternalInput").ap()
    wu = nc.dram_tensor("wu", [NIT, 128, 4096], F32, kind="ExternalInput").ap()
    wd = nc.dram_tensor("wd", [NJT, 128, 11008], F32, kind="ExternalInput").ap()
    toko = nc.dram_tensor("toko", [1, 1], I32, kind="ExternalInput").ap()
    outfm = nc.dram_tensor("outfm", [4096, 1024], F32, kind="ExternalOutput").ap()

    bcd = nc.dram_tensor("bcd", [32, 512], F32).ap()
    wgbf = nc.dram_tensor("wgbf", [NIT, 128, 4096], BF16).ap()
    wubf = nc.dram_tensor("wubf", [NIT, 128, 4096], BF16).ap()
    wdbf = nc.dram_tensor("wdbf", [NJT, 128, 11008], BF16).ap()
    cc_in = nc.dram_tensor("cc_in", [2048, 2048], F32)
    cc_out = nc.dram_tensor("cc_out", [4096, 2048], F32)
    gusc = nc.dram_tensor("gusc", [NIT, 128, 1024], BF16).ap()
    zsc = nc.dram_tensor("zsc", [4096, 1024], F32).ap()
    r2sc = nc.dram_tensor("r2sc", [4096, 1024], F32).ap()

    xv = xfm.rearrange("(kt p) t -> p kt t", p=P)
    xhv = xhh.rearrange("(j p) t -> p j t", p=P)
    ccin_v = cc_in.ap().rearrange("(j p) t -> p j t", p=P)
    ccout_v = cc_out.ap().rearrange("(kt p) t -> p kt t", p=P)
    gusc_v = gusc.rearrange("it p t -> p it t")

    def sidx(c, n, j):
        return c * 256 + n * 16 + j + 1

    def pidx(c, n, j):
        return c * (16 * GP_J) + n * GP_J + (j - SPLIT_J) + 1

    with ExitStack() as PS:
        E = PS.enter_context
        sem = {name: E(nc.semaphore(name)) for name in SEMS}
        block = E(nc.Block())

        xpw_bf = E(nc.sbuf_tensor("xpw_bf", [P, 5120], BF16))
        dtw_bf = E(nc.sbuf_tensor("dtw_bf", [P, 2048], BF16))
        dtb_sb = E(nc.sbuf_tensor("dtb_sb", [P, 16], F32))
        A_sb = E(nc.sbuf_tensor("A_sb", [P, 256], F32))
        d1_sb = E(nc.sbuf_tensor("d1_sb", [P, 16], F32))
        ones_sb = E(nc.sbuf_tensor("ones_sb", [P, 1], F32))
        onebc_sb = E(nc.sbuf_tensor("onebc_sb", [1, 128], F32))
        h_state = E(nc.sbuf_tensor("h_state", [P, 256], F32))
        stat4 = E(nc.sbuf_tensor("stat4", [1, 4 * T], F32))
        m_sb = stat4[0:1, 0:T]
        s2m_sb = stat4[0:1, T:2 * T]
        var_sb = stat4[0:1, 2 * T:3 * T]
        rstd_sb = stat4[0:1, 3 * T:4 * T]
        cst_sb = E(nc.sbuf_tensor("cst_sb", [P, 4], F32))

        @block.sync
        def _(sync):
            sync.dma_start(out=dtb_sb[:], in_=dtbi).then_inc(sem["LD"], 16)
            sync.dma_start(out=A_sb[:], in_=alog).then_inc(sem["LD"], 16)
            sync.dma_start(out=d1_sb[:], in_=dvec).then_inc(sem["LD"], 16)
            sync.dma_start(out=ones_sb[:], in_=ones1).then_inc(sem["LD"], 16)
            sync.dma_start(out=onebc_sb[:], in_=onebc).then_inc(sem["LD"], 16)

        @block.gpsimd
        def _(g):
            g.dma_start(out=xpw_bf[:], in_=xpwt).then_inc(sem["LDG"], 16)
            g.dma_start(out=dtw_bf[:], in_=dtwt).then_inc(sem["LDG"], 16)


        @block.scalar
        def _(s):
            s.wait_ge(sem["LD"], 80)
            s.activation(out=A_sb[:], in_=A_sb[:], func=AF.Exp).then_inc(sem["SETS"], 1)

        @block.vector
        def _(v):
            v.wait_ge(sem["SETS"], 1)
            v.tensor_scalar_mul(out=A_sb[:], in0=A_sb[:], scalar1=-1.0)
            v.tensor_scalar_add(out=d1_sb[:], in0=d1_sb[:], scalar1=1.0)
            v.memset(cst_sb[:, 0:1], 1.0)
            v.memset(cst_sb[:, 1:2], EPS)
            v.memset(cst_sb[:, 2:3], -0.5)
            v.memset(h_state[:], 0.0).then_inc(sem["SETV"], 1)

        with ExitStack() as SA:
            Ea = SA.enter_context
            xbf_sb = Ea(nc.sbuf_tensor("xbf_sb", [P, 8 * T], BF16))
            xh_sb = Ea(nc.sbuf_tensor("xh_sb", [P, DTl * T], F32))
            dt_sb = Ea(nc.sbuf_tensor("dt_sb", [P, DTl * T], F32))
            y_sb = Ea(nc.sbuf_tensor("y_sb", [P, DTl * T], F32))
            dtin_bf = Ea(nc.sbuf_tensor("dtin_bf", [P, T], BF16))
            bc_sb = Ea(nc.sbuf_tensor("bc_sb", [32, T], F32))
            e1_sb = Ea(nc.sbuf_tensor("e1_sb", [P, T], F32))
            B_sb = [Ea(nc.sbuf_tensor(f"B_sb{k}", [P, T], F32)) for k in range(2)]
            C_sb = [Ea(nc.sbuf_tensor(f"C_sb{k}", [P, T], F32)) for k in range(2)]
            dA_sb = [Ea(nc.sbuf_tensor(f"dA_sb{k}", [P, T + 1], F32)) for k in range(2)]
            dBxB = [Ea(nc.sbuf_tensor(f"dBxB{k}", [P, 4 * (T + 1)], F32)) for k in range(3)]
            hrB = [Ea(nc.sbuf_tensor(f"hrB{k}", [P, 4 * (T + 1)], F32)) for k in range(3)]
            hcB = Ea(nc.sbuf_tensor("hcB", [P, 4 * T], F32))
            ps_xp1 = Ea(nc.psum_tensor("ps_xp1", [P, T], F32))
            ps_xp2 = Ea(nc.psum_tensor("ps_xp2", [32, T], F32))
            ps_dt = [Ea(nc.psum_tensor(f"ps_dt{k}", [P, T], F32)) for k in range(2)]

            xbf_v = xbf_sb[:].rearrange("p (kt t) -> p kt t", t=T)
            xh_v = xh_sb[:].rearrange("p (j t) -> p j t", t=T)
            y_v = y_sb[:].rearrange("p (j t) -> p j t", t=T)

            def gbi(c, n, jq):
                return c * 64 + n * 4 + jq + 1

            @block.gpsimd
            def _(g):
                g.memset(dA_sb[0][:], 0.0)
                g.memset(dA_sb[1][:], 0.0)
                g.memset(dBxB[0][:], 0.0)
                g.memset(dBxB[2][:], 0.0)
                g.memset(dBxB[1][:], 0.0).then_inc(sem["SETG"], 1)

            for c in range(NCH):
                t0 = c * T

                @block.sync
                def _(sync, c=c, t0=t0):
                    if c > 0:
                        sync.wait_ge(sem["GB"], c * 64)
                    for q in range(2):
                        sync.dma_start(
                            out=xh_v[:, q * 8:(q + 1) * 8, :],
                            in_=xhv[:, q * 8:(q + 1) * 8, t0:t0 + T],
                        ).then_inc(sem["XH"], 16)
                    sync.wait_ge(sem["BCV"], c + 1)
                    sync.dma_start(out=bcd, in_=bc_sb[:]).then_inc(sem["BCD"], 16)
                    sync.wait_ge(sem["BCD"], (c + 1) * 16)
                    for n in range(NS):
                        N = c * 16 + n
                        if N >= 2:
                            sync.wait_ge(sem["GB"], (N - 1) * 4)
                        sync.dma_start(out=B_sb[n % 2][:],
                                       in_=bcd[n:n + 1, :].broadcast_to([P, T])).then_inc(sem["CSB"], 16)
                        if N >= 2:
                            sync.wait_ge(sem["HCB"], (N - 1) * 4)
                        sync.dma_start(out=C_sb[n % 2][:],
                                       in_=bcd[16 + n:17 + n, :].broadcast_to([P, T])).then_inc(sem["CSB"], 16)
                    sync.wait_ge(sem["YA"], (c + 1) * 64)
                    for q in range(2):
                        sync.dma_start(
                            out=ccin_v[:, q * 8:(q + 1) * 8, t0:t0 + T],
                            in_=y_v[:, q * 8:(q + 1) * 8, :],
                        ).then_inc(sem["R1"], 16)

                @block.gpsimd
                def _(g, c=c, t0=t0):
                    for grp in range(4):
                        if grp == 0:
                            if c > 0:
                                g.wait_ge(sem["XP"], c)
                        else:
                            g.wait_ge(sem["XPH"], c * 3 + grp)
                        g.dma_start(
                            out=xbf_v[:, 0:8, :],
                            in_=xv[:, grp * 8:(grp + 1) * 8, t0:t0 + T],
                        ).then_inc(sem["XBF"], 16)
                    ppb = [0, 8, 15, 22, 29, 36, 43, 51, 58, 65, 72, 79, 86]

                    def emit_dbx(kb):
                        n, jq = kb // 4, kb % 4
                        j0 = jq * 4
                        gbg = gbi(c, n, jq)
                        if jq == 0:
                            g.wait_ge(sem["CSB"], (c * 16 + n + 1) * 32)
                        if gbg >= 4:
                            g.wait_ge(sem["SC"], (gbg - 3) * 4)
                        g.wait_ge(sem["YI"], c * 16 + j0 + 4)
                        g.tensor_tensor(
                            out=dBxB[(gbg - 1) % 3][:].rearrange("p (q t) -> p q t", t=T + 1)[:, :, 1:T + 1],
                            in0=xh_sb[:, j0 * T:(j0 + 4) * T].rearrange("p (q t) -> p q t", t=T),
                            in1=B_sb[n % 2][:].unsqueeze(1).broadcast_to([P, 4, T]),
                            op=ALU.mult).then_inc(sem["GB"], 1)

                    def emit_hc(kb):
                        gbp = gbi(c, kb // 4, kb % 4)
                        npp = kb // 4
                        g.wait_ge(sem["SC"], gbp * 4)
                        if gbp >= 2:
                            g.wait_ge(sem["YA"], gbp - 1)
                        g.tensor_tensor(
                            out=hcB[:].rearrange("p (q t) -> p q t", t=T),
                            in0=hrB[(gbp - 1) % 3][:].rearrange("p (q t) -> p q t", t=T + 1)[:, :, 1:T + 1],
                            in1=C_sb[npp % 2][:].unsqueeze(1).broadcast_to([P, 4, T]),
                            op=ALU.mult).then_inc(sem["HCB"], 1)

                    emit_dbx(0)
                    emit_dbx(1)
                    for kb in range(64):
                        if kb in (16, 32, 48):
                            p = c * 3 + kb // 16 - 1
                            i0, i1 = ppb[p], ppb[p + 1]
                            g.dma_start(out=wgbf[i0:i1], in_=wg[i0:i1]).then_inc(sem["PP"], 16)
                            g.dma_start(out=wubf[i0:i1], in_=wu[i0:i1]).then_inc(sem["PP"], 16)
                        if c < 2 and kb in (8, 24, 40, 56):
                            p = c * 4 + kb // 16
                            g.dma_start(out=wdbf[p * 4:(p + 1) * 4], in_=wd[p * 4:(p + 1) * 4]).then_inc(sem["PPD"], 16)
                        if kb + 2 < 64:
                            emit_dbx(kb + 2)
                        emit_hc(kb)

                @block.tensor
                def _(tn, c=c):
                    if c == 0:
                        tn.wait_ge(sem["LDG"], 32)
                    else:
                        tn.wait_ge(sem["DTIN"], c)
                        tn.wait_ge(sem["BCV"], c)
                    for grp in range(4):
                        tn.wait_ge(sem["XBF"], c * 64 + (grp + 1) * 16)
                        for kt in range(8):
                            kg = grp * 8 + kt
                            st = kg == 0
                            sp = kg == 31
                            tn.matmul(ps_xp1[:], xpw_bf[:, kg * 160:kg * 160 + 128],
                                      xbf_sb[:, kt * T:(kt + 1) * T], start=st, stop=sp)
                            ins = tn.matmul(ps_xp2[:], xpw_bf[:, kg * 160 + 128:kg * 160 + 160],
                                            xbf_sb[:, kt * T:(kt + 1) * T], start=st, stop=sp)
                        if grp < 3:
                            ins.then_inc(sem["XPH"], 1)
                        else:
                            ins.then_inc(sem["XP"], 1)
                    tn.wait_ge(sem["DTIN"], c + 1)
                    for j in range(DTl):
                        if c * 16 + j >= 2:
                            tn.wait_ge(sem["SP"], c * 16 + j - 1)
                        tn.matmul(ps_dt[j % 2][:], dtw_bf[:, j * 128:(j + 1) * 128],
                                  dtin_bf[:], start=True, stop=True).then_inc(sem["DTM"], 1)

                @block.scalar
                def _(s, c=c):
                    if c == 0:
                        s.wait_ge(sem["SETV"], 1)
                        s.wait_ge(sem["SETG"], 1)
                    s.wait_ge(sem["XP"], c + 1)
                    s.activation(out=dtin_bf[:], in_=ps_xp1[:], func=AF.Copy).then_inc(sem["DTIN"], 1)
                    for j in range(DTl):
                        s.wait_ge(sem["DTM"], c * 16 + j + 1)
                        if c >= 1:
                            s.wait_ge(sem["YI"], (c - 1) * 16 + j + 1)
                        s.activation(out=e1_sb[:], in_=ps_dt[j % 2][:], func=AF.Exp,
                                     bias=dtb_sb[:, j:j + 1])
                        s.activation(out=dt_sb[:, j * T:(j + 1) * T], in_=e1_sb[:],
                                     func=AF.Ln, bias=cst_sb[:, 0:1]).then_inc(sem["SP"], 1)
                    for n in range(NS):
                        N = c * 16 + n
                        for j in range(DTl):
                            I = sidx(c, n, j)
                            if c >= 1 and j % 4 == 0:
                                jq = j // 4
                                gbg = gbi(c, n, jq)
                                if gbg >= 4:
                                    s.wait_ge(sem["SC"], (gbg - 3) * 4)
                                s.activation(
                                    out=dBxB[(gbg - 1) % 3][:].rearrange("p (q t) -> p q t", t=T + 1)[:, :, 0:1].squeeze(2),
                                    in_=h_state[:, (jq * 64 + n):(jq * 64 + n + 49):16],
                                    func=AF.Copy).then_inc(sem["CP"], 1)
                            if I >= 3:
                                s.wait_ge(sem["SC"], I - 2)
                            s.activation(out=dA_sb[(I - 1) % 2][:, 1:T + 1],
                                         in_=dt_sb[:, j * T:(j + 1) * T], func=AF.Exp,
                                         scale=A_sb[:, (j * 16 + n):(j * 16 + n) + 1]
                                         ).then_inc(sem["DA"], 1)
                            if I >= 3:
                                cp, kp = divmod(I - 3, 256)
                                if cp <= 2:
                                    npp, jp = kp // 16, kp % 16
                                    seg = kp % 4 if False else ((I - 3) % 4)
                                    s.wait_ge(sem["SC"], I - 2)
                                    s.activation(out=h_state[:, (jp * 16 + npp):(jp * 16 + npp) + 1],
                                                 in_=hrB[((I - 3) // 4) % 3][:, seg * (T + 1) + T: seg * (T + 1) + T + 1],
                                                 func=AF.Copy).then_inc(sem["HS"], 1)

                @block.vector
                def _(v, c=c):
                    v.wait_ge(sem["XP"], c + 1)
                    if c > 0:
                        v.wait_ge(sem["BCD"], c * 16)
                    v.tensor_copy(out=bc_sb[:], in_=ps_xp2[:]).then_inc(sem["BCV"], 1)
                    v.wait_ge(sem["XH"], (c + 1) * 32)
                    if c > 0:
                        v.wait_ge(sem["R1"], c * 32)
                    for j in range(DTl):
                        v.tensor_scalar(out=y_sb[:, j * T:(j + 1) * T],
                                        in0=xh_sb[:, j * T:(j + 1) * T],
                                        scalar1=d1_sb[:, j:j + 1], scalar2=None,
                                        op0=ALU.mult)
                        v.wait_ge(sem["SP"], c * 16 + j + 1)
                        v.tensor_tensor(out=xh_sb[:, j * T:(j + 1) * T],
                                        in0=dt_sb[:, j * T:(j + 1) * T],
                                        in1=xh_sb[:, j * T:(j + 1) * T],
                                        op=ALU.mult).then_inc(sem["YI"], 1)
                    for kb in range(64):
                        n, jq = kb // 4, kb % 4
                        gbg = gbi(c, n, jq)
                        for q in range(4):
                            k = kb * 4 + q
                            I = c * 256 + k + 1
                            v.wait_ge(sem["DA"], I)
                            v.wait_ge(sem["GB"], gbg)
                            if c >= 1:
                                v.wait_ge(sem["CP"], (gbg - 64))
                            if gbg >= 4:
                                v.wait_ge(sem["HCB"], gbg - 3)
                            if I >= 13:
                                v.wait_ge(sem["HS"], min(I - 12, 3 * 256))
                            seg = (I - 1) % 4
                            v.tensor_tensor_scan(
                                out=hrB[(gbg - 1) % 3][:, seg * (T + 1):(seg + 1) * (T + 1)],
                                data0=dA_sb[(I - 1) % 2][:],
                                data1=dBxB[(gbg - 1) % 3][:, seg * (T + 1):(seg + 1) * (T + 1)],
                                initial=0.0,
                                op0=ALU.mult, op1=ALU.add).then_inc(sem["SC"], 1)
                        if kb >= 1:
                            gbp = gbg - 1
                            j0p = ((kb - 1) % 4) * 4
                            v.wait_ge(sem["HCB"], gbp)
                            v.tensor_tensor(out=y_sb[:, j0p * T:(j0p + 4) * T],
                                            in0=y_sb[:, j0p * T:(j0p + 4) * T],
                                            in1=hcB[:],
                                            op=ALU.add).then_inc(sem["YA"], 1)
                    gbp = gbi(c, 15, 3)
                    v.wait_ge(sem["HCB"], gbp)
                    v.tensor_tensor(out=y_sb[:, 12 * T:16 * T],
                                    in0=y_sb[:, 12 * T:16 * T],
                                    in1=hcB[:],
                                    op=ALU.add).then_inc(sem["YA"], 1)

        @block.gpsimd
        def _(g):
            g.wait_ge(sem["R1"], NCH * 32)
            g.collective_compute(
                "AllGather", ALU.bypass,
                replica_groups=[[0, 1], [2, 3], [4, 5], [6, 7]],
                ins=[cc_in.ap().opt()], outs=[cc_out.ap().opt()],
            ).then_inc(sem["CC"], 1)

        with ExitStack() as SC:
            Ec = SC.enter_context
            z_bf = Ec(nc.sbuf_tensor("z_bf", [P, NKT * 1024], BF16))

            with ExitStack() as SC1:
                E1 = SC1.enter_context
                r1_sb = E1(nc.sbuf_tensor("r1_sb", [P, NKT * T], F32))
                sq_sb = [E1(nc.sbuf_tensor(f"sq_sb{k}", [P, T], F32)) for k in range(2)]
                z32_sb = [E1(nc.sbuf_tensor(f"z32_sb{k}", [P, T], F32)) for k in range(2)]
                t1_sb = E1(nc.sbuf_tensor("t1_sb", [P, T], F32))
                ps_s1 = E1(nc.psum_tensor("ps_s1", [1, T], F32))
                ps_s2 = E1(nc.psum_tensor("ps_s2", [1, T], F32))
                ps_mb = E1(nc.psum_tensor("ps_mb", [P, T], F32))
                ps_rb = E1(nc.psum_tensor("ps_rb", [P, T], F32))

                r1_v = r1_sb[:].rearrange("p (kt t) -> p kt t", t=T)

                for tc in range(2):
                    @block.gpsimd
                    def _(g, tc=tc):
                        g.wait_ge(sem["CC"], 1)
                        if tc > 0:
                            g.wait_ge(sem["ZV"], tc * 32)
                        with g.register(f"rtok{tc}") as rtok:
                            g.reg_load(rtok, toko[0:1, 0:1])
                            voff = g.snap(rtok, min_val=0, max_val=1024)
                            for q in range(4):
                                g.dma_start(
                                    out=r1_v[:, q * 8:(q + 1) * 8, :],
                                    in_=ccout_v[:, q * 8:(q + 1) * 8, bass.ds(voff + tc * T, T)],
                                ).then_inc(sem["LNL"], 16)

                    @block.scalar
                    def _(s, tc=tc):
                        for kt in range(NKT):
                            k = tc * 32 + kt
                            s.wait_ge(sem["LNL"], tc * 64 + (kt // 8 + 1) * 16)
                            if k >= 2:
                                s.wait_ge(sem["S12"], k - 1)
                            s.activation(out=sq_sb[kt % 2][:],
                                         in_=r1_sb[:, kt * T:(kt + 1) * T],
                                         func=AF.Square).then_inc(sem["SQ"], 1)

                    @block.tensor
                    def _(tn, tc=tc):
                        if tc > 0:
                            tn.wait_ge(sem["STV"], tc)
                        for kt in range(NKT):
                            k = tc * 32 + kt
                            tn.wait_ge(sem["LNL"], tc * 64 + (kt // 8 + 1) * 16)
                            tn.matmul(ps_s1[:], ones_sb[:], r1_sb[:, kt * T:(kt + 1) * T],
                                      start=kt == 0, stop=kt == NKT - 1)
                            tn.wait_ge(sem["SQ"], k + 1)
                            tn.matmul(ps_s2[:], ones_sb[:], sq_sb[kt % 2][:],
                                      start=kt == 0, stop=kt == NKT - 1).then_inc(sem["S12"], 1)
                        tn.wait_ge(sem["STS"], tc + 1)
                        if tc > 0:
                            tn.wait_ge(sem["ZV"], tc * 32)
                        tn.matmul(ps_mb[:], onebc_sb[:], m_sb, start=True, stop=True)
                        tn.matmul(ps_rb[:], onebc_sb[:], rstd_sb, start=True, stop=True).then_inc(sem["MBC"], 1)

                    @block.vector
                    def _(v, tc=tc):
                        v.wait_ge(sem["S12"], (tc + 1) * 32)
                        v.tensor_scalar_mul(out=m_sb, in0=ps_s1[:], scalar1=1.0 / 4096)
                        v.tensor_scalar_mul(out=s2m_sb, in0=ps_s2[:], scalar1=1.0 / 4096)
                        v.tensor_tensor(out=var_sb, in0=m_sb, in1=m_sb, op=ALU.mult)
                        v.tensor_tensor(out=var_sb, in0=s2m_sb, in1=var_sb,
                                        op=ALU.subtract).then_inc(sem["STV"], 1)
                        for kt in range(NKT):
                            k = tc * 32 + kt
                            v.wait_ge(sem["MBC"], tc + 1)
                            if k >= 2:
                                v.wait_ge(sem["ZBF"], k - 1)
                                v.wait_ge(sem["ZSC"], (k - 1) * 16)
                            v.tensor_tensor(out=t1_sb[:], in0=r1_sb[:, kt * T:(kt + 1) * T],
                                            in1=ps_mb[:], op=ALU.subtract)
                            v.tensor_tensor(out=z32_sb[kt % 2][:], in0=t1_sb[:],
                                            in1=ps_rb[:], op=ALU.mult).then_inc(sem["ZV"], 1)

                    @block.scalar
                    def _(s, tc=tc):
                        s.wait_ge(sem["STV"], tc + 1)
                        s.activation(out=var_sb, in_=var_sb, func=AF.Ln,
                                     bias=cst_sb[0:1, 1:2])
                        s.activation(out=rstd_sb, in_=var_sb, func=AF.Exp,
                                     scale=cst_sb[0:1, 2:3]).then_inc(sem["STS"], 1)
                        for kt in range(NKT):
                            k = tc * 32 + kt
                            s.wait_ge(sem["ZV"], k + 1)
                            s.activation(out=z_bf[:, kt * 1024 + tc * T: kt * 1024 + tc * T + T],
                                         in_=z32_sb[kt % 2][:], func=AF.Copy).then_inc(sem["ZBF"], 1)

                    @block.sync
                    def _(sync, tc=tc):
                        for kt in range(NKT):
                            k = tc * 32 + kt
                            sync.wait_ge(sem["ZV"], k + 1)
                            sync.dma_start(
                                out=zsc[kt * 128:(kt + 1) * 128, tc * T:(tc + 1) * T],
                                in_=z32_sb[kt % 2][:]).then_inc(sem["ZSC"], 16)

            with ExitStack() as SC2:
                E2 = SC2.enter_context
                wg_bf = [E2(nc.sbuf_tensor(f"wg_bf{k}", [P, 4096], BF16)) for k in range(2)]
                wu_bf = [E2(nc.sbuf_tensor(f"wu_bf{k}", [P, 4096], BF16)) for k in range(2)]
                sg_sb = [E2(nc.sbuf_tensor(f"sg_sb{k}", [P, T], F32)) for k in range(2)]
                gu_sb = [E2(nc.sbuf_tensor(f"gu_sb{k}", [P, 1024], BF16)) for k in range(2)]
                ps_g = [E2(nc.psum_tensor(f"ps_g{k}", [P, T], F32)) for k in range(2)]
                ps_u = [E2(nc.psum_tensor(f"ps_u{k}", [P, T], F32)) for k in range(2)]

                @block.scalar
                def _(s):
                    s.wait_ge(sem["PP"], 24 * 16)
                    s.wait_ge(sem["ZBF"], 64)
                    s.wait_ge(sem["ZSC"], 64 * 16)
                    for i in range(NIT):
                        if i >= 2:
                            s.wait_ge(sem["MMG"], 2 * (i - 1))
                        s.dma_start(out=wg_bf[i % 2][:], in_=wgbf[i]).then_inc(sem["WG"], 16)
                        if i >= 2:
                            s.wait_ge(sem["MMU"], 2 * (i - 1))
                        s.dma_start(out=wu_bf[i % 2][:], in_=wubf[i]).then_inc(sem["WU"], 16)
                        if i >= 1:
                            for tc in range(2):
                                q = 2 * (i - 1) + tc
                                s.wait_ge(sem["MMG"], q + 1)
                                if q >= 2:
                                    s.wait_ge(sem["GUV"], q - 1)
                                s.activation(out=sg_sb[tc % 2][:], in_=ps_g[tc][:],
                                             func=AF.Silu).then_inc(sem["SIL"], 1)
                    for tc in range(2):
                        q = 2 * (NIT - 1) + tc
                        s.wait_ge(sem["MMG"], q + 1)
                        s.wait_ge(sem["GUV"], q - 1)
                        s.activation(out=sg_sb[tc % 2][:], in_=ps_g[tc][:],
                                     func=AF.Silu).then_inc(sem["SIL"], 1)

                @block.tensor
                def _(tn):
                    tn.wait_ge(sem["ZBF"], 64)
                    for i in range(NIT):
                        tn.wait_ge(sem["WG"], (i + 1) * 16)
                        for tc in range(2):
                            if 2 * i + tc >= 2:
                                tn.wait_ge(sem["SIL"], 2 * i + tc - 1)
                            for kt in range(NKT):
                                ins = tn.matmul(
                                    ps_g[tc][:], wg_bf[i % 2][:, kt * 128:(kt + 1) * 128],
                                    z_bf[:, kt * 1024 + tc * T: kt * 1024 + tc * T + T],
                                    start=kt == 0, stop=kt == NKT - 1)
                            ins.then_inc(sem["MMG"], 1)
                        tn.wait_ge(sem["WU"], (i + 1) * 16)
                        for tc in range(2):
                            if 2 * i + tc >= 2:
                                tn.wait_ge(sem["GUV"], 2 * i + tc - 1)
                            for kt in range(NKT):
                                ins = tn.matmul(
                                    ps_u[tc][:], wu_bf[i % 2][:, kt * 128:(kt + 1) * 128],
                                    z_bf[:, kt * 1024 + tc * T: kt * 1024 + tc * T + T],
                                    start=kt == 0, stop=kt == NKT - 1)
                            ins.then_inc(sem["MMU"], 1)

                @block.vector
                def _(v):
                    for i in range(NIT):
                        for tc in range(2):
                            q = 2 * i + tc
                            v.wait_ge(sem["SIL"], q + 1)
                            v.wait_ge(sem["MMU"], q + 1)
                            if i >= 2:
                                v.wait_ge(sem["GUS"], (i - 1) * 16)
                            v.tensor_tensor(out=gu_sb[i % 2][:, tc * T:(tc + 1) * T],
                                            in0=sg_sb[tc % 2][:], in1=ps_u[tc][:],
                                            op=ALU.mult).then_inc(sem["GUV"], 1)

                @block.sync
                def _(sync):
                    for i in range(NIT):
                        sync.wait_ge(sem["GUV"], 2 * i + 2)
                        sync.dma_start(out=gusc[i], in_=gu_sb[i % 2][:]).then_inc(sem["GUS"], 16)

        with ExitStack() as SD:
            Ed = SD.enter_context
            gu2_sb = Ed(nc.sbuf_tensor("gu2_sb", [P, NIT * T], BF16))
            wd_bf = [Ed(nc.sbuf_tensor(f"wd_bf{k}", [P, NIT * 128], BF16)) for k in range(2)]
            zj_sb = [Ed(nc.sbuf_tensor(f"zj_sb{k}", [P, T], F32)) for k in range(4)]
            r2_sb = [Ed(nc.sbuf_tensor(f"r2_sb{k}", [P, T], F32)) for k in range(2)]
            sq2_sb = [Ed(nc.sbuf_tensor(f"sq2_sb{k}", [P, T], F32)) for k in range(2)]
            rl_sb = [Ed(nc.sbuf_tensor(f"rl_sb{k}", [P, T], F32)) for k in range(2)]
            o32_sb = [Ed(nc.sbuf_tensor(f"o32_sb{k}", [P, T], F32)) for k in range(2)]
            t2_sb = Ed(nc.sbuf_tensor("t2_sb", [P, T], F32))
            ps_d = [Ed(nc.psum_tensor(f"ps_d{k}", [P, T], F32)) for k in range(2)]
            ps_s3 = Ed(nc.psum_tensor("ps_s3", [1, T], F32))
            ps_s4 = Ed(nc.psum_tensor("ps_s4", [1, T], F32))
            ps_m2 = Ed(nc.psum_tensor("ps_m2", [P, T], F32))
            ps_r2 = Ed(nc.psum_tensor("ps_r2", [P, T], F32))

            gu2_v = gu2_sb[:].rearrange("p (it t) -> p it t", t=T)

            for tc in range(2):
                @block.sync
                def _(sync, tc=tc):
                    if tc == 0:
                        sync.wait_ge(sem["GUS"], NIT * 16)
                        sync.wait_ge(sem["ZSC"], 64 * 16)
                    else:
                        sync.wait_ge(sem["MMD"], tc * 32)
                    for q in range(11):
                        i0 = q * 8
                        i1 = min(i0 + 8, NIT)
                        sync.dma_start(
                            out=gu2_v[:, i0:i1, :],
                            in_=gusc_v[:, i0:i1, tc * T:(tc + 1) * T],
                        ).then_inc(sem["GUL"], 16)
                    for j in range(NJT):
                        JD = tc * 32 + j
                        sync.wait_ge(sem["R2V"], JD + 1)
                        sync.dma_start(out=r2sc[j * 128:(j + 1) * 128, tc * T:(tc + 1) * T],
                                       in_=r2_sb[j % 2][:]).then_inc(sem["R2S"], 16)
                    for j in range(NJT):
                        JD = tc * 32 + j
                        if JD >= 2:
                            sync.wait_ge(sem["OV"], JD - 1)
                        sync.dma_start(out=rl_sb[j % 2][:],
                                       in_=r2sc[j * 128:(j + 1) * 128, tc * T:(tc + 1) * T]
                                       ).then_inc(sem["R2L"], 16)
                        sync.wait_ge(sem["OV"], JD + 1)
                        sync.dma_start(out=outfm[j * 128:(j + 1) * 128, tc * T:(tc + 1) * T],
                                       in_=o32_sb[j % 2][:]).then_inc(sem["OUT"], 16)

                @block.gpsimd
                def _(g, tc=tc):
                    if tc == 0:
                        g.wait_ge(sem["GUS"], NIT * 16)
                        g.wait_ge(sem["PPD"], 128)
                    for j in range(NJT):
                        JD = tc * 32 + j
                        if JD >= 2:
                            g.wait_ge(sem["MMD"], JD - 1)
                        g.dma_start(out=wd_bf[j % 2][:], in_=wdbf[j]).then_inc(sem["WD"], 16)

                @block.tensor
                def _(tn, tc=tc):
                    tn.wait_ge(sem["GUL"], tc * 176 + 176)
                    if tc > 0:
                        tn.wait_ge(sem["ST2V"], tc)
                    for j in range(NJT):
                        JD = tc * 32 + j
                        tn.wait_ge(sem["WD"], (JD + 1) * 16)
                        if JD >= 2:
                            tn.wait_ge(sem["R2V"], JD - 1)
                        for i in range(NIT):
                            ins = tn.matmul(ps_d[j % 2][:],
                                            wd_bf[j % 2][:, i * 128:(i + 1) * 128],
                                            gu2_sb[:, i * T:(i + 1) * T],
                                            start=i == 0, stop=i == NIT - 1)
                        ins.then_inc(sem["MMD"], 1)
                        if j >= 1:
                            jp = j - 1
                            JP = tc * 32 + jp
                            tn.wait_ge(sem["R2V"], JP + 1)
                            tn.matmul(ps_s3[:], ones_sb[:], r2_sb[jp % 2][:],
                                      start=jp == 0, stop=False)
                            tn.wait_ge(sem["SQ2"], JP + 1)
                            tn.matmul(ps_s4[:], ones_sb[:], sq2_sb[jp % 2][:],
                                      start=jp == 0, stop=False).then_inc(sem["S34"], 1)
                    jp = NJT - 1
                    JP = tc * 32 + jp
                    tn.wait_ge(sem["R2V"], JP + 1)
                    tn.matmul(ps_s3[:], ones_sb[:], r2_sb[jp % 2][:],
                              start=False, stop=True)
                    tn.wait_ge(sem["SQ2"], JP + 1)
                    tn.matmul(ps_s4[:], ones_sb[:], sq2_sb[jp % 2][:],
                              start=False, stop=True).then_inc(sem["S34"], 1)
                    tn.wait_ge(sem["ST2S"], tc + 1)
                    if tc > 0:
                        tn.wait_ge(sem["OV"], tc * 32)
                    tn.matmul(ps_m2[:], onebc_sb[:], m_sb, start=True, stop=True)
                    tn.matmul(ps_r2[:], onebc_sb[:], rstd_sb, start=True, stop=True).then_inc(sem["M2BC"], 1)

                @block.vector
                def _(v, tc=tc):
                    for j in range(NJT):
                        JD = tc * 32 + j
                        v.wait_ge(sem["MMD"], JD + 1)
                        v.wait_ge(sem["ZJL"], (JD + 1) * 16)
                        if JD >= 2:
                            v.wait_ge(sem["SQ2"], JD - 1)
                            v.wait_ge(sem["S34"], JD - 1)
                            v.wait_ge(sem["R2S"], (JD - 1) * 16)
                        v.tensor_tensor(out=r2_sb[j % 2][:], in0=ps_d[j % 2][:],
                                        in1=zj_sb[j % 4][:], op=ALU.add).then_inc(sem["R2V"], 1)
                    v.wait_ge(sem["S34"], (tc + 1) * 32)
                    v.tensor_scalar_mul(out=m_sb, in0=ps_s3[:], scalar1=1.0 / 4096)
                    v.tensor_scalar_mul(out=s2m_sb, in0=ps_s4[:], scalar1=1.0 / 4096)
                    v.tensor_tensor(out=var_sb, in0=m_sb, in1=m_sb, op=ALU.mult)
                    v.tensor_tensor(out=var_sb, in0=s2m_sb, in1=var_sb,
                                    op=ALU.subtract).then_inc(sem["ST2V"], 1)
                    for j in range(NJT):
                        JD = tc * 32 + j
                        v.wait_ge(sem["M2BC"], tc + 1)
                        v.wait_ge(sem["R2L"], (JD + 1) * 16)
                        if JD >= 2:
                            v.wait_ge(sem["OUT"], (JD - 1) * 16)
                        v.tensor_tensor(out=t2_sb[:], in0=rl_sb[j % 2][:],
                                        in1=ps_m2[:], op=ALU.subtract)
                        v.tensor_tensor(out=o32_sb[j % 2][:], in0=t2_sb[:],
                                        in1=ps_r2[:], op=ALU.mult).then_inc(sem["OV"], 1)

                @block.scalar
                def _(s, tc=tc):
                    if tc == 0:
                        s.wait_ge(sem["ZSC"], 64 * 16)
                    for j in range(3):
                        JD = tc * 32 + j
                        if JD >= 4:
                            s.wait_ge(sem["R2V"], JD - 3)
                        s.dma_start(out=zj_sb[j % 4][:],
                                    in_=zsc[j * 128:(j + 1) * 128, tc * T:(tc + 1) * T]
                                    ).then_inc(sem["ZJL"], 16)
                    for j in range(NJT):
                        JD = tc * 32 + j
                        if j + 3 < NJT:
                            if JD >= 1:
                                s.wait_ge(sem["R2V"], JD)
                            s.dma_start(out=zj_sb[(j + 3) % 4][:],
                                        in_=zsc[(j + 3) * 128:(j + 4) * 128, tc * T:(tc + 1) * T]
                                        ).then_inc(sem["ZJL"], 16)
                        s.wait_ge(sem["R2V"], JD + 1)
                        if JD >= 2:
                            s.wait_ge(sem["S34"], JD - 1)
                        s.activation(out=sq2_sb[j % 2][:], in_=r2_sb[j % 2][:],
                                     func=AF.Square).then_inc(sem["SQ2"], 1)
                    s.wait_ge(sem["ST2V"], tc + 1)
                    s.activation(out=var_sb, in_=var_sb, func=AF.Ln,
                                 bias=cst_sb[0:1, 1:2])
                    s.activation(out=rstd_sb, in_=var_sb, func=AF.Exp,
                                 scale=cst_sb[0:1, 2:3]).then_inc(sem["ST2S"], 1)

        @block.gpsimd
        def _(g):
            g.wait_ge(sem["OUT"], 64 * 16)

    return nc


_NC_CACHE = None


def kernel(L0_attn_x, L0_attn_A_log, L0_attn_D, L0_attn_x_proj_w,
           L0_attn_dt_proj_w, L0_attn_dt_proj_b, L0_ffn_w_gate,
           L0_ffn_w_up, L0_ffn_w_down):
    global _NC_CACHE
    x = np.asarray(L0_attn_x, np.float32)
    A_log = np.asarray(L0_attn_A_log, np.float32)
    Dv = np.asarray(L0_attn_D, np.float32)
    xpw = np.asarray(L0_attn_x_proj_w, np.float32)
    dtw = np.asarray(L0_attn_dt_proj_w, np.float32)
    dtb = np.asarray(L0_attn_dt_proj_b, np.float32)
    wgate = np.asarray(L0_ffn_w_gate, np.float32)
    wup = np.asarray(L0_ffn_w_up, np.float32)
    wdown = np.asarray(L0_ffn_w_down, np.float32)

    xfm = np.ascontiguousarray(x.transpose(0, 2, 1))
    xpwt = np.ascontiguousarray(
        xpw.T.reshape(32, 128, 160).transpose(1, 0, 2)).reshape(128, 5120)
    wg_t = np.ascontiguousarray(
        wgate.reshape(86, 128, 32, 128).transpose(0, 3, 2, 1)).reshape(86, 128, 4096)
    wu_t = np.ascontiguousarray(
        wup.reshape(86, 128, 32, 128).transpose(0, 3, 2, 1)).reshape(86, 128, 4096)
    wd_t = np.ascontiguousarray(
        wdown.reshape(32, 128, 86, 128).transpose(0, 3, 2, 1)).reshape(32, 128, 11008)
    ones1 = np.ones((128, 1), np.float32)
    onebc = np.ones((1, 128), np.float32)

    in_maps = []
    for c in range(8):
        b, half = c // 2, c % 2
        d0 = half * 2048
        in_maps.append(dict(
            xfm=xfm[b],
            xhh=np.ascontiguousarray(xfm[b, d0:d0 + 2048, :]),
            xpwt=xpwt,
            dtwt=np.ascontiguousarray(dtw[d0:d0 + 2048].T),
            dtbi=np.ascontiguousarray(dtb[d0:d0 + 2048].reshape(16, 128).T),
            alog=np.ascontiguousarray(
                A_log[d0:d0 + 2048].reshape(16, 128, 16).transpose(1, 0, 2)).reshape(128, 256),
            dvec=np.ascontiguousarray(Dv[d0:d0 + 2048].reshape(16, 128).T),
            ones1=ones1, onebc=onebc,
            wg=wg_t, wu=wu_t, wd=wd_t,
            toko=np.array([[half * 1024]], np.int32),
        ))

    if _NC_CACHE is None:
        _NC_CACHE = build()
    res = run_bass_kernel_spmd(_NC_CACHE, in_maps, core_ids=list(range(8)))

    out = np.empty((4, 2048, 4096), np.float32)
    for c in range(8):
        b, half = c // 2, c % 2
        out[b, half * 1024:(half + 1) * 1024, :] = res.results[c]["outfm"].T
    return out


# version 12
# speedup vs baseline: 1.1513x; 1.1513x over previous
import numpy as np
from contextlib import ExitStack

import concourse.bass as bass
import concourse.mybir as mybir
from concourse.bass_utils import run_bass_kernel_spmd

F32 = mybir.dt.float32
BF16 = mybir.dt.bfloat16
I32 = mybir.dt.int32
AF = mybir.ActivationFunctionType
ALU = mybir.AluOpType

P = 128
T = 512
NCH = 4
DTl = 16
NS = 16
SPLIT_J = 2
NKT = 32
NIT = 86
NJT = 32
EPS = 1e-5
GP_J = DTl - SPLIT_J

SEMS = [
    "LD", "LDG", "SETS", "SETV",
    "XBF", "XH", "XP", "DTIN", "BCV", "DTM", "SP", "YI",
    "BC", "CSB", "DA", "SC", "HC", "VCH", "R1", "CC", "GB", "YA", "HS", "SETG",
    "CP", "HCB", "XPH", "BCD", "PP", "PPD",
    "LNL", "SQ", "S12", "STV", "STS", "MBC", "ZV", "ZBF", "ZSC",
    "WG", "WU", "MMG", "MMU", "SIL", "GUV", "GUS",
    "GUL", "WD", "MMD", "ZJL", "R2V", "SQ2", "S34", "R2S",
    "ST2V", "ST2S", "M2BC", "R2L", "OV", "OUT",
]


def build():
    nc = bass.Bass("TRN2", target_bir_lowering=False)

    xfm = nc.dram_tensor("xfm", [4096, 2048], F32, kind="ExternalInput").ap()
    xhh = nc.dram_tensor("xhh", [2048, 2048], F32, kind="ExternalInput").ap()
    xpwt = nc.dram_tensor("xpwt", [128, 5120], F32, kind="ExternalInput").ap()
    dtwt = nc.dram_tensor("dtwt", [128, 2048], F32, kind="ExternalInput").ap()
    dtbi = nc.dram_tensor("dtbi", [128, 16], F32, kind="ExternalInput").ap()
    alog = nc.dram_tensor("alog", [128, 256], F32, kind="ExternalInput").ap()
    dvec = nc.dram_tensor("dvec", [128, 16], F32, kind="ExternalInput").ap()
    ones1 = nc.dram_tensor("ones1", [128, 1], F32, kind="ExternalInput").ap()
    onebc = nc.dram_tensor("onebc", [1, 128], F32, kind="ExternalInput").ap()
    wg = nc.dram_tensor("wg", [NIT, 128, 4096], F32, kind="ExternalInput").ap()
    wu = nc.dram_tensor("wu", [NIT, 128, 4096], F32, kind="ExternalInput").ap()
    wd = nc.dram_tensor("wd", [NJT, 128, 11008], F32, kind="ExternalInput").ap()
    toko = nc.dram_tensor("toko", [1, 1], I32, kind="ExternalInput").ap()
    outfm = nc.dram_tensor("outfm", [4096, 1024], F32, kind="ExternalOutput").ap()

    bcd = nc.dram_tensor("bcd", [32, 512], F32).ap()
    wgbf = nc.dram_tensor("wgbf", [NIT, 128, 4096], BF16).ap()
    wubf = nc.dram_tensor("wubf", [NIT, 128, 4096], BF16).ap()
    wdbf = nc.dram_tensor("wdbf", [NJT, 128, 11008], BF16).ap()
    cc_in = nc.dram_tensor("cc_in", [2048, 2048], F32)
    cc_out = nc.dram_tensor("cc_out", [4096, 2048], F32)
    gusc = nc.dram_tensor("gusc", [NIT, 128, 1024], BF16).ap()
    zsc = nc.dram_tensor("zsc", [4096, 1024], F32).ap()
    r2sc = nc.dram_tensor("r2sc", [4096, 1024], F32).ap()

    xv = xfm.rearrange("(kt p) t -> p kt t", p=P)
    xhv = xhh.rearrange("(j p) t -> p j t", p=P)
    ccin_v = cc_in.ap().rearrange("(j p) t -> p j t", p=P)
    ccout_v = cc_out.ap().rearrange("(kt p) t -> p kt t", p=P)
    gusc_v = gusc.rearrange("it p t -> p it t")

    def sidx(c, n, j):
        return c * 256 + n * 16 + j + 1

    def pidx(c, n, j):
        return c * (16 * GP_J) + n * GP_J + (j - SPLIT_J) + 1

    with ExitStack() as PS:
        E = PS.enter_context
        sem = {name: E(nc.semaphore(name)) for name in SEMS}
        block = E(nc.Block())

        xpw_bf = E(nc.sbuf_tensor("xpw_bf", [P, 5120], BF16))
        dtw_bf = E(nc.sbuf_tensor("dtw_bf", [P, 2048], BF16))
        dtb_sb = E(nc.sbuf_tensor("dtb_sb", [P, 16], F32))
        A_sb = E(nc.sbuf_tensor("A_sb", [P, 256], F32))
        d1_sb = E(nc.sbuf_tensor("d1_sb", [P, 16], F32))
        ones_sb = E(nc.sbuf_tensor("ones_sb", [P, 1], F32))
        onebc_sb = E(nc.sbuf_tensor("onebc_sb", [1, 128], F32))
        h_state = E(nc.sbuf_tensor("h_state", [P, 256], F32))
        cst_sb = E(nc.sbuf_tensor("cst_sb", [P, 4], F32))

        @block.sync
        def _(sync):
            sync.dma_start(out=dtb_sb[:], in_=dtbi).then_inc(sem["LD"], 16)
            sync.dma_start(out=A_sb[:], in_=alog).then_inc(sem["LD"], 16)
            sync.dma_start(out=d1_sb[:], in_=dvec).then_inc(sem["LD"], 16)
            sync.dma_start(out=ones_sb[:], in_=ones1).then_inc(sem["LD"], 16)
            sync.dma_start(out=onebc_sb[:], in_=onebc).then_inc(sem["LD"], 16)

        @block.gpsimd
        def _(g):
            g.dma_start(out=xpw_bf[:], in_=xpwt).then_inc(sem["LDG"], 16)
            g.dma_start(out=dtw_bf[:], in_=dtwt).then_inc(sem["LDG"], 16)


        @block.scalar
        def _(s):
            s.wait_ge(sem["LD"], 80)
            s.activation(out=A_sb[:], in_=A_sb[:], func=AF.Exp).then_inc(sem["SETS"], 1)

        @block.vector
        def _(v):
            v.wait_ge(sem["SETS"], 1)
            v.tensor_scalar_mul(out=A_sb[:], in0=A_sb[:], scalar1=-1.0)
            v.tensor_scalar_add(out=d1_sb[:], in0=d1_sb[:], scalar1=1.0)
            v.memset(cst_sb[:, 0:1], 1.0)
            v.memset(cst_sb[:, 1:2], EPS)
            v.memset(cst_sb[:, 2:3], -0.5)
            v.memset(h_state[:], 0.0).then_inc(sem["SETV"], 1)

        with ExitStack() as SA:
            Ea = SA.enter_context
            xbf_sb = Ea(nc.sbuf_tensor("xbf_sb", [P, 8 * T], BF16))
            xh_sb = Ea(nc.sbuf_tensor("xh_sb", [P, DTl * T], F32))
            dt_sb = Ea(nc.sbuf_tensor("dt_sb", [P, DTl * T], F32))
            y_sb = Ea(nc.sbuf_tensor("y_sb", [P, DTl * T], F32))
            dtin_bf = Ea(nc.sbuf_tensor("dtin_bf", [P, T], BF16))
            bc_sb = Ea(nc.sbuf_tensor("bc_sb", [32, T], F32))
            e1_sb = Ea(nc.sbuf_tensor("e1_sb", [P, T], F32))
            B_sb = [Ea(nc.sbuf_tensor(f"B_sb{k}", [P, T], F32)) for k in range(2)]
            C_sb = [Ea(nc.sbuf_tensor(f"C_sb{k}", [P, T], F32)) for k in range(2)]
            dA_sb = [Ea(nc.sbuf_tensor(f"dA_sb{k}", [P, T + 1], F32)) for k in range(2)]
            dBxB = [Ea(nc.sbuf_tensor(f"dBxB{k}", [P, 4 * (T + 1)], F32)) for k in range(3)]
            hrB = [Ea(nc.sbuf_tensor(f"hrB{k}", [P, 4 * (T + 1)], F32)) for k in range(3)]
            hcB = [Ea(nc.sbuf_tensor(f"hcB{k}", [P, 4 * T], F32)) for k in range(2)]
            ps_xp1 = Ea(nc.psum_tensor("ps_xp1", [P, T], F32))
            ps_xp2 = Ea(nc.psum_tensor("ps_xp2", [32, T], F32))
            ps_dt = [Ea(nc.psum_tensor(f"ps_dt{k}", [P, T], F32)) for k in range(2)]

            xbf_v = xbf_sb[:].rearrange("p (kt t) -> p kt t", t=T)
            xh_v = xh_sb[:].rearrange("p (j t) -> p j t", t=T)
            y_v = y_sb[:].rearrange("p (j t) -> p j t", t=T)

            def gbi(c, n, jq):
                return c * 64 + n * 4 + jq + 1

            @block.gpsimd
            def _(g):
                g.memset(dA_sb[0][:], 0.0)
                g.memset(dA_sb[1][:], 0.0)
                g.memset(dBxB[0][:], 0.0)
                g.memset(dBxB[2][:], 0.0)
                g.memset(dBxB[1][:], 0.0).then_inc(sem["SETG"], 1)

            for c in range(NCH):
                t0 = c * T

                @block.sync
                def _(sync, c=c, t0=t0):
                    if c > 0:
                        sync.wait_ge(sem["GB"], c * 64)
                    for q in range(2):
                        sync.dma_start(
                            out=xh_v[:, q * 8:(q + 1) * 8, :],
                            in_=xhv[:, q * 8:(q + 1) * 8, t0:t0 + T],
                        ).then_inc(sem["XH"], 16)
                    sync.wait_ge(sem["BCV"], c + 1)
                    sync.dma_start(out=bcd, in_=bc_sb[:]).then_inc(sem["BCD"], 16)
                    sync.wait_ge(sem["BCD"], (c + 1) * 16)
                    for n in range(NS):
                        N = c * 16 + n
                        if N >= 2:
                            sync.wait_ge(sem["GB"], (N - 1) * 4)
                        sync.dma_start(out=B_sb[n % 2][:],
                                       in_=bcd[n:n + 1, :].broadcast_to([P, T])).then_inc(sem["CSB"], 16)
                        if N >= 2:
                            sync.wait_ge(sem["HCB"], (N - 1) * 4)
                        sync.dma_start(out=C_sb[n % 2][:],
                                       in_=bcd[16 + n:17 + n, :].broadcast_to([P, T])).then_inc(sem["CSB"], 16)
                    sync.wait_ge(sem["YA"], (c + 1) * 64)
                    for q in range(2):
                        sync.dma_start(
                            out=ccin_v[:, q * 8:(q + 1) * 8, t0:t0 + T],
                            in_=y_v[:, q * 8:(q + 1) * 8, :],
                        ).then_inc(sem["R1"], 16)

                @block.gpsimd
                def _(g, c=c, t0=t0):
                    for grp in range(4):
                        if grp == 0:
                            if c > 0:
                                g.wait_ge(sem["XP"], c)
                        else:
                            g.wait_ge(sem["XPH"], c * 3 + grp)
                        g.dma_start(
                            out=xbf_v[:, 0:8, :],
                            in_=xv[:, grp * 8:(grp + 1) * 8, t0:t0 + T],
                        ).then_inc(sem["XBF"], 16)
                    ppb = [0, 8, 15, 22, 29, 36, 43, 51, 58, 65, 72, 79, 86]

                    def emit_dbx(kb):
                        n, jq = kb // 4, kb % 4
                        j0 = jq * 4
                        gbg = gbi(c, n, jq)
                        if jq == 0:
                            g.wait_ge(sem["CSB"], (c * 16 + n + 1) * 32)
                        if gbg >= 4:
                            g.wait_ge(sem["SC"], (gbg - 3) * 4)
                        g.wait_ge(sem["YI"], c * 16 + j0 + 4)
                        g.tensor_tensor(
                            out=dBxB[(gbg - 1) % 3][:].rearrange("p (q t) -> p q t", t=T + 1)[:, :, 1:T + 1],
                            in0=xh_sb[:, j0 * T:(j0 + 4) * T].rearrange("p (q t) -> p q t", t=T),
                            in1=B_sb[n % 2][:].unsqueeze(1).broadcast_to([P, 4, T]),
                            op=ALU.mult).then_inc(sem["GB"], 1)

                    def emit_hc(kb):
                        gbp = gbi(c, kb // 4, kb % 4)
                        npp = kb // 4
                        g.wait_ge(sem["SC"], gbp * 4)
                        if gbp >= 3:
                            g.wait_ge(sem["YA"], gbp - 2)
                        g.tensor_tensor(
                            out=hcB[(gbp - 1) % 2][:].rearrange("p (q t) -> p q t", t=T),
                            in0=hrB[(gbp - 1) % 3][:].rearrange("p (q t) -> p q t", t=T + 1)[:, :, 1:T + 1],
                            in1=C_sb[npp % 2][:].unsqueeze(1).broadcast_to([P, 4, T]),
                            op=ALU.mult).then_inc(sem["HCB"], 1)

                    emit_dbx(0)
                    emit_dbx(1)
                    for kb in range(64):
                        if kb in (16, 32, 48):
                            p = c * 3 + kb // 16 - 1
                            i0, i1 = ppb[p], ppb[p + 1]
                            g.dma_start(out=wgbf[i0:i1], in_=wg[i0:i1]).then_inc(sem["PP"], 16)
                            g.dma_start(out=wubf[i0:i1], in_=wu[i0:i1]).then_inc(sem["PP"], 16)
                        if c < 2 and kb in (8, 24, 40, 56):
                            p = c * 4 + kb // 16
                            g.dma_start(out=wdbf[p * 4:(p + 1) * 4], in_=wd[p * 4:(p + 1) * 4]).then_inc(sem["PPD"], 16)
                        if kb + 2 < 64:
                            emit_dbx(kb + 2)
                        emit_hc(kb)

                @block.tensor
                def _(tn, c=c):
                    if c == 0:
                        tn.wait_ge(sem["LDG"], 32)
                    else:
                        tn.wait_ge(sem["DTIN"], c)
                        tn.wait_ge(sem["BCV"], c)
                    for grp in range(4):
                        tn.wait_ge(sem["XBF"], c * 64 + (grp + 1) * 16)
                        for kt in range(8):
                            kg = grp * 8 + kt
                            st = kg == 0
                            sp = kg == 31
                            tn.matmul(ps_xp1[:], xpw_bf[:, kg * 160:kg * 160 + 128],
                                      xbf_sb[:, kt * T:(kt + 1) * T], start=st, stop=sp)
                            ins = tn.matmul(ps_xp2[:], xpw_bf[:, kg * 160 + 128:kg * 160 + 160],
                                            xbf_sb[:, kt * T:(kt + 1) * T], start=st, stop=sp)
                        if grp < 3:
                            ins.then_inc(sem["XPH"], 1)
                        else:
                            ins.then_inc(sem["XP"], 1)
                    tn.wait_ge(sem["DTIN"], c + 1)
                    for j in range(DTl):
                        if c * 16 + j >= 2:
                            tn.wait_ge(sem["SP"], c * 16 + j - 1)
                        tn.matmul(ps_dt[j % 2][:], dtw_bf[:, j * 128:(j + 1) * 128],
                                  dtin_bf[:], start=True, stop=True).then_inc(sem["DTM"], 1)

                @block.scalar
                def _(s, c=c):
                    if c == 0:
                        s.wait_ge(sem["SETV"], 1)
                        s.wait_ge(sem["SETG"], 1)
                    s.wait_ge(sem["XP"], c + 1)
                    s.activation(out=dtin_bf[:], in_=ps_xp1[:], func=AF.Copy).then_inc(sem["DTIN"], 1)
                    for j in range(DTl):
                        s.wait_ge(sem["DTM"], c * 16 + j + 1)
                        if c >= 1:
                            s.wait_ge(sem["YI"], (c - 1) * 16 + j + 1)
                        s.activation(out=e1_sb[:], in_=ps_dt[j % 2][:], func=AF.Exp,
                                     bias=dtb_sb[:, j:j + 1])
                        s.activation(out=dt_sb[:, j * T:(j + 1) * T], in_=e1_sb[:],
                                     func=AF.Ln, bias=cst_sb[:, 0:1]).then_inc(sem["SP"], 1)
                    for n in range(NS):
                        N = c * 16 + n
                        for j in range(DTl):
                            I = sidx(c, n, j)
                            if c >= 1 and j % 4 == 0:
                                jq = j // 4
                                gbg = gbi(c, n, jq)
                                if gbg >= 4:
                                    s.wait_ge(sem["SC"], (gbg - 3) * 4)
                                s.activation(
                                    out=dBxB[(gbg - 1) % 3][:].rearrange("p (q t) -> p q t", t=T + 1)[:, :, 0:1].squeeze(2),
                                    in_=h_state[:, (jq * 64 + n):(jq * 64 + n + 49):16],
                                    func=AF.Copy).then_inc(sem["CP"], 1)
                            if I >= 3:
                                s.wait_ge(sem["SC"], I - 2)
                            s.activation(out=dA_sb[(I - 1) % 2][:, 1:T + 1],
                                         in_=dt_sb[:, j * T:(j + 1) * T], func=AF.Exp,
                                         scale=A_sb[:, (j * 16 + n):(j * 16 + n) + 1]
                                         ).then_inc(sem["DA"], 1)
                            if I >= 3:
                                cp, kp = divmod(I - 3, 256)
                                if cp <= 2:
                                    npp, jp = kp // 16, kp % 16
                                    seg = kp % 4 if False else ((I - 3) % 4)
                                    s.wait_ge(sem["SC"], I - 2)
                                    s.activation(out=h_state[:, (jp * 16 + npp):(jp * 16 + npp) + 1],
                                                 in_=hrB[((I - 3) // 4) % 3][:, seg * (T + 1) + T: seg * (T + 1) + T + 1],
                                                 func=AF.Copy).then_inc(sem["HS"], 1)

                @block.vector
                def _(v, c=c):
                    v.wait_ge(sem["XP"], c + 1)
                    if c > 0:
                        v.wait_ge(sem["BCD"], c * 16)
                    v.tensor_copy(out=bc_sb[:], in_=ps_xp2[:]).then_inc(sem["BCV"], 1)
                    v.wait_ge(sem["XH"], (c + 1) * 32)
                    if c > 0:
                        v.wait_ge(sem["R1"], c * 32)
                    for j in range(DTl):
                        v.tensor_scalar(out=y_sb[:, j * T:(j + 1) * T],
                                        in0=xh_sb[:, j * T:(j + 1) * T],
                                        scalar1=d1_sb[:, j:j + 1], scalar2=None,
                                        op0=ALU.mult)
                        v.wait_ge(sem["SP"], c * 16 + j + 1)
                        v.tensor_tensor(out=xh_sb[:, j * T:(j + 1) * T],
                                        in0=dt_sb[:, j * T:(j + 1) * T],
                                        in1=xh_sb[:, j * T:(j + 1) * T],
                                        op=ALU.mult).then_inc(sem["YI"], 1)
                    for kb in range(64):
                        n, jq = kb // 4, kb % 4
                        gbg = gbi(c, n, jq)
                        for q in range(4):
                            k = kb * 4 + q
                            I = c * 256 + k + 1
                            v.wait_ge(sem["DA"], I)
                            v.wait_ge(sem["GB"], gbg)
                            if c >= 1:
                                v.wait_ge(sem["CP"], (gbg - 64))
                            if gbg >= 4:
                                v.wait_ge(sem["HCB"], gbg - 3)
                            if I >= 13:
                                v.wait_ge(sem["HS"], min(I - 12, 3 * 256))
                            seg = (I - 1) % 4
                            v.tensor_tensor_scan(
                                out=hrB[(gbg - 1) % 3][:, seg * (T + 1):(seg + 1) * (T + 1)],
                                data0=dA_sb[(I - 1) % 2][:],
                                data1=dBxB[(gbg - 1) % 3][:, seg * (T + 1):(seg + 1) * (T + 1)],
                                initial=0.0,
                                op0=ALU.mult, op1=ALU.add).then_inc(sem["SC"], 1)
                        if kb >= 1:
                            gbp = gbg - 1
                            j0p = ((kb - 1) % 4) * 4
                            v.wait_ge(sem["HCB"], gbp)
                            v.tensor_tensor(out=y_sb[:, j0p * T:(j0p + 4) * T],
                                            in0=y_sb[:, j0p * T:(j0p + 4) * T],
                                            in1=hcB[(gbp - 1) % 2][:],
                                            op=ALU.add).then_inc(sem["YA"], 1)
                    gbp = gbi(c, 15, 3)
                    v.wait_ge(sem["HCB"], gbp)
                    v.tensor_tensor(out=y_sb[:, 12 * T:16 * T],
                                    in0=y_sb[:, 12 * T:16 * T],
                                    in1=hcB[(gbp - 1) % 2][:],
                                    op=ALU.add).then_inc(sem["YA"], 1)

        @block.gpsimd
        def _(g):
            g.wait_ge(sem["R1"], NCH * 32)
            g.collective_compute(
                "AllGather", ALU.bypass,
                replica_groups=[[0, 1], [2, 3], [4, 5], [6, 7]],
                ins=[cc_in.ap().opt()], outs=[cc_out.ap().opt()],
            ).then_inc(sem["CC"], 1)

        with ExitStack() as SC:
            Ec = SC.enter_context
            z_bf = Ec(nc.sbuf_tensor("z_bf", [P, NKT * 1024], BF16))

            with ExitStack() as SC1:
                E1 = SC1.enter_context
                r1_sb = E1(nc.sbuf_tensor("r1_sb", [P, NKT * T], F32))
                sq_sb = [E1(nc.sbuf_tensor(f"sq_sb{k}", [P, T], F32)) for k in range(2)]
                z32_sb = [E1(nc.sbuf_tensor(f"z32_sb{k}", [P, T], F32)) for k in range(2)]
                t1_sb = E1(nc.sbuf_tensor("t1_sb", [P, T], F32))
                stat4 = E1(nc.sbuf_tensor("stat4", [1, 4 * T], F32))
                m_sb = stat4[0:1, 0:T]
                s2m_sb = stat4[0:1, T:2 * T]
                var_sb = stat4[0:1, 2 * T:3 * T]
                rstd_sb = stat4[0:1, 3 * T:4 * T]
                ps_s1 = E1(nc.psum_tensor("ps_s1", [1, T], F32))
                ps_s2 = E1(nc.psum_tensor("ps_s2", [1, T], F32))
                ps_mb = E1(nc.psum_tensor("ps_mb", [P, T], F32))
                ps_rb = E1(nc.psum_tensor("ps_rb", [P, T], F32))

                r1_v = r1_sb[:].rearrange("p (kt t) -> p kt t", t=T)

                for tc in range(2):
                    @block.gpsimd
                    def _(g, tc=tc):
                        g.wait_ge(sem["CC"], 1)
                        if tc > 0:
                            g.wait_ge(sem["ZV"], tc * 32)
                        with g.register(f"rtok{tc}") as rtok:
                            g.reg_load(rtok, toko[0:1, 0:1])
                            voff = g.snap(rtok, min_val=0, max_val=1024)
                            for q in range(4):
                                g.dma_start(
                                    out=r1_v[:, q * 8:(q + 1) * 8, :],
                                    in_=ccout_v[:, q * 8:(q + 1) * 8, bass.ds(voff + tc * T, T)],
                                ).then_inc(sem["LNL"], 16)

                    @block.scalar
                    def _(s, tc=tc):
                        for kt in range(NKT):
                            k = tc * 32 + kt
                            s.wait_ge(sem["LNL"], tc * 64 + (kt // 8 + 1) * 16)
                            if k >= 2:
                                s.wait_ge(sem["S12"], k - 1)
                            s.activation(out=sq_sb[kt % 2][:],
                                         in_=r1_sb[:, kt * T:(kt + 1) * T],
                                         func=AF.Square).then_inc(sem["SQ"], 1)

                    @block.tensor
                    def _(tn, tc=tc):
                        if tc > 0:
                            tn.wait_ge(sem["STV"], tc)
                        for kt in range(NKT):
                            k = tc * 32 + kt
                            tn.wait_ge(sem["LNL"], tc * 64 + (kt // 8 + 1) * 16)
                            tn.matmul(ps_s1[:], ones_sb[:], r1_sb[:, kt * T:(kt + 1) * T],
                                      start=kt == 0, stop=kt == NKT - 1)
                            tn.wait_ge(sem["SQ"], k + 1)
                            tn.matmul(ps_s2[:], ones_sb[:], sq_sb[kt % 2][:],
                                      start=kt == 0, stop=kt == NKT - 1).then_inc(sem["S12"], 1)
                        tn.wait_ge(sem["STS"], tc + 1)
                        if tc > 0:
                            tn.wait_ge(sem["ZV"], tc * 32)
                        tn.matmul(ps_mb[:], onebc_sb[:], m_sb, start=True, stop=True)
                        tn.matmul(ps_rb[:], onebc_sb[:], rstd_sb, start=True, stop=True).then_inc(sem["MBC"], 1)

                    @block.vector
                    def _(v, tc=tc):
                        v.wait_ge(sem["S12"], (tc + 1) * 32)
                        v.tensor_scalar_mul(out=m_sb, in0=ps_s1[:], scalar1=1.0 / 4096)
                        v.tensor_scalar_mul(out=s2m_sb, in0=ps_s2[:], scalar1=1.0 / 4096)
                        v.tensor_tensor(out=var_sb, in0=m_sb, in1=m_sb, op=ALU.mult)
                        v.tensor_tensor(out=var_sb, in0=s2m_sb, in1=var_sb,
                                        op=ALU.subtract).then_inc(sem["STV"], 1)
                        for kt in range(NKT):
                            k = tc * 32 + kt
                            v.wait_ge(sem["MBC"], tc + 1)
                            if k >= 2:
                                v.wait_ge(sem["ZBF"], k - 1)
                                v.wait_ge(sem["ZSC"], (k - 1) * 16)
                            v.tensor_tensor(out=t1_sb[:], in0=r1_sb[:, kt * T:(kt + 1) * T],
                                            in1=ps_mb[:], op=ALU.subtract)
                            v.tensor_tensor(out=z32_sb[kt % 2][:], in0=t1_sb[:],
                                            in1=ps_rb[:], op=ALU.mult).then_inc(sem["ZV"], 1)

                    @block.scalar
                    def _(s, tc=tc):
                        s.wait_ge(sem["STV"], tc + 1)
                        s.activation(out=var_sb, in_=var_sb, func=AF.Ln,
                                     bias=cst_sb[0:1, 1:2])
                        s.activation(out=rstd_sb, in_=var_sb, func=AF.Exp,
                                     scale=cst_sb[0:1, 2:3]).then_inc(sem["STS"], 1)
                        for kt in range(NKT):
                            k = tc * 32 + kt
                            s.wait_ge(sem["ZV"], k + 1)
                            s.activation(out=z_bf[:, kt * 1024 + tc * T: kt * 1024 + tc * T + T],
                                         in_=z32_sb[kt % 2][:], func=AF.Copy).then_inc(sem["ZBF"], 1)

                    @block.sync
                    def _(sync, tc=tc):
                        for kt in range(NKT):
                            k = tc * 32 + kt
                            sync.wait_ge(sem["ZV"], k + 1)
                            sync.dma_start(
                                out=zsc[kt * 128:(kt + 1) * 128, tc * T:(tc + 1) * T],
                                in_=z32_sb[kt % 2][:]).then_inc(sem["ZSC"], 16)

            with ExitStack() as SC2:
                E2 = SC2.enter_context
                wg_bf = [E2(nc.sbuf_tensor(f"wg_bf{k}", [P, 4096], BF16)) for k in range(2)]
                wu_bf = [E2(nc.sbuf_tensor(f"wu_bf{k}", [P, 4096], BF16)) for k in range(2)]
                sg_sb = [E2(nc.sbuf_tensor(f"sg_sb{k}", [P, T], F32)) for k in range(2)]
                gu_sb = [E2(nc.sbuf_tensor(f"gu_sb{k}", [P, 1024], BF16)) for k in range(2)]
                ps_g = [E2(nc.psum_tensor(f"ps_g{k}", [P, T], F32)) for k in range(2)]
                ps_u = [E2(nc.psum_tensor(f"ps_u{k}", [P, T], F32)) for k in range(2)]

                @block.scalar
                def _(s):
                    s.wait_ge(sem["PP"], 24 * 16)
                    s.wait_ge(sem["ZBF"], 64)
                    s.wait_ge(sem["ZSC"], 64 * 16)
                    for i in range(NIT):
                        if i >= 2:
                            s.wait_ge(sem["MMG"], 2 * (i - 1))
                        s.dma_start(out=wg_bf[i % 2][:], in_=wgbf[i]).then_inc(sem["WG"], 16)
                        if i >= 2:
                            s.wait_ge(sem["MMU"], 2 * (i - 1))
                        s.dma_start(out=wu_bf[i % 2][:], in_=wubf[i]).then_inc(sem["WU"], 16)
                        if i >= 1:
                            for tc in range(2):
                                q = 2 * (i - 1) + tc
                                s.wait_ge(sem["MMG"], q + 1)
                                if q >= 2:
                                    s.wait_ge(sem["GUV"], q - 1)
                                s.activation(out=sg_sb[tc % 2][:], in_=ps_g[tc][:],
                                             func=AF.Silu).then_inc(sem["SIL"], 1)
                    for tc in range(2):
                        q = 2 * (NIT - 1) + tc
                        s.wait_ge(sem["MMG"], q + 1)
                        s.wait_ge(sem["GUV"], q - 1)
                        s.activation(out=sg_sb[tc % 2][:], in_=ps_g[tc][:],
                                     func=AF.Silu).then_inc(sem["SIL"], 1)

                @block.tensor
                def _(tn):
                    tn.wait_ge(sem["ZBF"], 64)
                    for i in range(NIT):
                        tn.wait_ge(sem["WG"], (i + 1) * 16)
                        for tc in range(2):
                            if 2 * i + tc >= 2:
                                tn.wait_ge(sem["SIL"], 2 * i + tc - 1)
                            for kt in range(NKT):
                                ins = tn.matmul(
                                    ps_g[tc][:], wg_bf[i % 2][:, kt * 128:(kt + 1) * 128],
                                    z_bf[:, kt * 1024 + tc * T: kt * 1024 + tc * T + T],
                                    start=kt == 0, stop=kt == NKT - 1)
                            ins.then_inc(sem["MMG"], 1)
                        tn.wait_ge(sem["WU"], (i + 1) * 16)
                        for tc in range(2):
                            if 2 * i + tc >= 2:
                                tn.wait_ge(sem["GUV"], 2 * i + tc - 1)
                            for kt in range(NKT):
                                ins = tn.matmul(
                                    ps_u[tc][:], wu_bf[i % 2][:, kt * 128:(kt + 1) * 128],
                                    z_bf[:, kt * 1024 + tc * T: kt * 1024 + tc * T + T],
                                    start=kt == 0, stop=kt == NKT - 1)
                            ins.then_inc(sem["MMU"], 1)

                @block.vector
                def _(v):
                    for i in range(NIT):
                        for tc in range(2):
                            q = 2 * i + tc
                            v.wait_ge(sem["SIL"], q + 1)
                            v.wait_ge(sem["MMU"], q + 1)
                            if i >= 2:
                                v.wait_ge(sem["GUS"], (i - 1) * 16)
                            v.tensor_tensor(out=gu_sb[i % 2][:, tc * T:(tc + 1) * T],
                                            in0=sg_sb[tc % 2][:], in1=ps_u[tc][:],
                                            op=ALU.mult).then_inc(sem["GUV"], 1)

                @block.sync
                def _(sync):
                    for i in range(NIT):
                        sync.wait_ge(sem["GUV"], 2 * i + 2)
                        sync.dma_start(out=gusc[i], in_=gu_sb[i % 2][:]).then_inc(sem["GUS"], 16)

        with ExitStack() as SD:
            Ed = SD.enter_context
            gu2_sb = Ed(nc.sbuf_tensor("gu2_sb", [P, NIT * T], BF16))
            wd_bf = [Ed(nc.sbuf_tensor(f"wd_bf{k}", [P, NIT * 128], BF16)) for k in range(2)]
            zj_sb = [Ed(nc.sbuf_tensor(f"zj_sb{k}", [P, T], F32)) for k in range(4)]
            r2_sb = [Ed(nc.sbuf_tensor(f"r2_sb{k}", [P, T], F32)) for k in range(2)]
            sq2_sb = [Ed(nc.sbuf_tensor(f"sq2_sb{k}", [P, T], F32)) for k in range(2)]
            rl_sb = [Ed(nc.sbuf_tensor(f"rl_sb{k}", [P, T], F32)) for k in range(2)]
            o32_sb = [Ed(nc.sbuf_tensor(f"o32_sb{k}", [P, T], F32)) for k in range(2)]
            t2_sb = Ed(nc.sbuf_tensor("t2_sb", [P, T], F32))
            stat4b = Ed(nc.sbuf_tensor("stat4b", [1, 4 * T], F32))
            m_sb = stat4b[0:1, 0:T]
            s2m_sb = stat4b[0:1, T:2 * T]
            var_sb = stat4b[0:1, 2 * T:3 * T]
            rstd_sb = stat4b[0:1, 3 * T:4 * T]
            ps_d = [Ed(nc.psum_tensor(f"ps_d{k}", [P, T], F32)) for k in range(2)]
            ps_s3 = Ed(nc.psum_tensor("ps_s3", [1, T], F32))
            ps_s4 = Ed(nc.psum_tensor("ps_s4", [1, T], F32))
            ps_m2 = Ed(nc.psum_tensor("ps_m2", [P, T], F32))
            ps_r2 = Ed(nc.psum_tensor("ps_r2", [P, T], F32))

            gu2_v = gu2_sb[:].rearrange("p (it t) -> p it t", t=T)

            for tc in range(2):
                @block.sync
                def _(sync, tc=tc):
                    if tc == 0:
                        sync.wait_ge(sem["GUS"], NIT * 16)
                        sync.wait_ge(sem["ZSC"], 64 * 16)
                    else:
                        sync.wait_ge(sem["MMD"], tc * 32)
                    for q in range(11):
                        i0 = q * 8
                        i1 = min(i0 + 8, NIT)
                        sync.dma_start(
                            out=gu2_v[:, i0:i1, :],
                            in_=gusc_v[:, i0:i1, tc * T:(tc + 1) * T],
                        ).then_inc(sem["GUL"], 16)
                    for j in range(NJT):
                        JD = tc * 32 + j
                        sync.wait_ge(sem["R2V"], JD + 1)
                        sync.dma_start(out=r2sc[j * 128:(j + 1) * 128, tc * T:(tc + 1) * T],
                                       in_=r2_sb[j % 2][:]).then_inc(sem["R2S"], 16)
                    for j in range(NJT):
                        JD = tc * 32 + j
                        if JD >= 2:
                            sync.wait_ge(sem["OV"], JD - 1)
                        sync.dma_start(out=rl_sb[j % 2][:],
                                       in_=r2sc[j * 128:(j + 1) * 128, tc * T:(tc + 1) * T]
                                       ).then_inc(sem["R2L"], 16)
                        sync.wait_ge(sem["OV"], JD + 1)
                        sync.dma_start(out=outfm[j * 128:(j + 1) * 128, tc * T:(tc + 1) * T],
                                       in_=o32_sb[j % 2][:]).then_inc(sem["OUT"], 16)

                @block.gpsimd
                def _(g, tc=tc):
                    if tc == 0:
                        g.wait_ge(sem["GUS"], NIT * 16)
                        g.wait_ge(sem["PPD"], 128)
                    for j in range(NJT):
                        JD = tc * 32 + j
                        if JD >= 2:
                            g.wait_ge(sem["MMD"], JD - 1)
                        g.dma_start(out=wd_bf[j % 2][:], in_=wdbf[j]).then_inc(sem["WD"], 16)

                @block.tensor
                def _(tn, tc=tc):
                    tn.wait_ge(sem["GUL"], tc * 176 + 176)
                    if tc > 0:
                        tn.wait_ge(sem["ST2V"], tc)
                    for j in range(NJT):
                        JD = tc * 32 + j
                        tn.wait_ge(sem["WD"], (JD + 1) * 16)
                        if JD >= 2:
                            tn.wait_ge(sem["R2V"], JD - 1)
                        for i in range(NIT):
                            ins = tn.matmul(ps_d[j % 2][:],
                                            wd_bf[j % 2][:, i * 128:(i + 1) * 128],
                                            gu2_sb[:, i * T:(i + 1) * T],
                                            start=i == 0, stop=i == NIT - 1)
                        ins.then_inc(sem["MMD"], 1)
                        if j >= 1:
                            jp = j - 1
                            JP = tc * 32 + jp
                            tn.wait_ge(sem["R2V"], JP + 1)
                            tn.matmul(ps_s3[:], ones_sb[:], r2_sb[jp % 2][:],
                                      start=jp == 0, stop=False)
                            tn.wait_ge(sem["SQ2"], JP + 1)
                            tn.matmul(ps_s4[:], ones_sb[:], sq2_sb[jp % 2][:],
                                      start=jp == 0, stop=False).then_inc(sem["S34"], 1)
                    jp = NJT - 1
                    JP = tc * 32 + jp
                    tn.wait_ge(sem["R2V"], JP + 1)
                    tn.matmul(ps_s3[:], ones_sb[:], r2_sb[jp % 2][:],
                              start=False, stop=True)
                    tn.wait_ge(sem["SQ2"], JP + 1)
                    tn.matmul(ps_s4[:], ones_sb[:], sq2_sb[jp % 2][:],
                              start=False, stop=True).then_inc(sem["S34"], 1)
                    tn.wait_ge(sem["ST2S"], tc + 1)
                    if tc > 0:
                        tn.wait_ge(sem["OV"], tc * 32)
                    tn.matmul(ps_m2[:], onebc_sb[:], m_sb, start=True, stop=True)
                    tn.matmul(ps_r2[:], onebc_sb[:], rstd_sb, start=True, stop=True).then_inc(sem["M2BC"], 1)

                @block.vector
                def _(v, tc=tc):
                    for j in range(NJT):
                        JD = tc * 32 + j
                        v.wait_ge(sem["MMD"], JD + 1)
                        v.wait_ge(sem["ZJL"], (JD + 1) * 16)
                        if JD >= 2:
                            v.wait_ge(sem["SQ2"], JD - 1)
                            v.wait_ge(sem["S34"], JD - 1)
                            v.wait_ge(sem["R2S"], (JD - 1) * 16)
                        v.tensor_tensor(out=r2_sb[j % 2][:], in0=ps_d[j % 2][:],
                                        in1=zj_sb[j % 4][:], op=ALU.add).then_inc(sem["R2V"], 1)
                    v.wait_ge(sem["S34"], (tc + 1) * 32)
                    v.tensor_scalar_mul(out=m_sb, in0=ps_s3[:], scalar1=1.0 / 4096)
                    v.tensor_scalar_mul(out=s2m_sb, in0=ps_s4[:], scalar1=1.0 / 4096)
                    v.tensor_tensor(out=var_sb, in0=m_sb, in1=m_sb, op=ALU.mult)
                    v.tensor_tensor(out=var_sb, in0=s2m_sb, in1=var_sb,
                                    op=ALU.subtract).then_inc(sem["ST2V"], 1)
                    for j in range(NJT):
                        JD = tc * 32 + j
                        v.wait_ge(sem["M2BC"], tc + 1)
                        v.wait_ge(sem["R2L"], (JD + 1) * 16)
                        if JD >= 2:
                            v.wait_ge(sem["OUT"], (JD - 1) * 16)
                        v.tensor_tensor(out=t2_sb[:], in0=rl_sb[j % 2][:],
                                        in1=ps_m2[:], op=ALU.subtract)
                        v.tensor_tensor(out=o32_sb[j % 2][:], in0=t2_sb[:],
                                        in1=ps_r2[:], op=ALU.mult).then_inc(sem["OV"], 1)

                @block.scalar
                def _(s, tc=tc):
                    if tc == 0:
                        s.wait_ge(sem["ZSC"], 64 * 16)
                    for j in range(3):
                        JD = tc * 32 + j
                        if JD >= 4:
                            s.wait_ge(sem["R2V"], JD - 3)
                        s.dma_start(out=zj_sb[j % 4][:],
                                    in_=zsc[j * 128:(j + 1) * 128, tc * T:(tc + 1) * T]
                                    ).then_inc(sem["ZJL"], 16)
                    for j in range(NJT):
                        JD = tc * 32 + j
                        if j + 3 < NJT:
                            if JD >= 1:
                                s.wait_ge(sem["R2V"], JD)
                            s.dma_start(out=zj_sb[(j + 3) % 4][:],
                                        in_=zsc[(j + 3) * 128:(j + 4) * 128, tc * T:(tc + 1) * T]
                                        ).then_inc(sem["ZJL"], 16)
                        s.wait_ge(sem["R2V"], JD + 1)
                        if JD >= 2:
                            s.wait_ge(sem["S34"], JD - 1)
                        s.activation(out=sq2_sb[j % 2][:], in_=r2_sb[j % 2][:],
                                     func=AF.Square).then_inc(sem["SQ2"], 1)
                    s.wait_ge(sem["ST2V"], tc + 1)
                    s.activation(out=var_sb, in_=var_sb, func=AF.Ln,
                                 bias=cst_sb[0:1, 1:2])
                    s.activation(out=rstd_sb, in_=var_sb, func=AF.Exp,
                                 scale=cst_sb[0:1, 2:3]).then_inc(sem["ST2S"], 1)

        @block.gpsimd
        def _(g):
            g.wait_ge(sem["OUT"], 64 * 16)

    return nc


_NC_CACHE = None


def kernel(L0_attn_x, L0_attn_A_log, L0_attn_D, L0_attn_x_proj_w,
           L0_attn_dt_proj_w, L0_attn_dt_proj_b, L0_ffn_w_gate,
           L0_ffn_w_up, L0_ffn_w_down):
    global _NC_CACHE
    x = np.asarray(L0_attn_x, np.float32)
    A_log = np.asarray(L0_attn_A_log, np.float32)
    Dv = np.asarray(L0_attn_D, np.float32)
    xpw = np.asarray(L0_attn_x_proj_w, np.float32)
    dtw = np.asarray(L0_attn_dt_proj_w, np.float32)
    dtb = np.asarray(L0_attn_dt_proj_b, np.float32)
    wgate = np.asarray(L0_ffn_w_gate, np.float32)
    wup = np.asarray(L0_ffn_w_up, np.float32)
    wdown = np.asarray(L0_ffn_w_down, np.float32)

    xfm = np.ascontiguousarray(x.transpose(0, 2, 1))
    xpwt = np.ascontiguousarray(
        xpw.T.reshape(32, 128, 160).transpose(1, 0, 2)).reshape(128, 5120)
    wg_t = np.ascontiguousarray(
        wgate.reshape(86, 128, 32, 128).transpose(0, 3, 2, 1)).reshape(86, 128, 4096)
    wu_t = np.ascontiguousarray(
        wup.reshape(86, 128, 32, 128).transpose(0, 3, 2, 1)).reshape(86, 128, 4096)
    wd_t = np.ascontiguousarray(
        wdown.reshape(32, 128, 86, 128).transpose(0, 3, 2, 1)).reshape(32, 128, 11008)
    ones1 = np.ones((128, 1), np.float32)
    onebc = np.ones((1, 128), np.float32)

    in_maps = []
    for c in range(8):
        b, half = c // 2, c % 2
        d0 = half * 2048
        in_maps.append(dict(
            xfm=xfm[b],
            xhh=np.ascontiguousarray(xfm[b, d0:d0 + 2048, :]),
            xpwt=xpwt,
            dtwt=np.ascontiguousarray(dtw[d0:d0 + 2048].T),
            dtbi=np.ascontiguousarray(dtb[d0:d0 + 2048].reshape(16, 128).T),
            alog=np.ascontiguousarray(
                A_log[d0:d0 + 2048].reshape(16, 128, 16).transpose(1, 0, 2)).reshape(128, 256),
            dvec=np.ascontiguousarray(Dv[d0:d0 + 2048].reshape(16, 128).T),
            ones1=ones1, onebc=onebc,
            wg=wg_t, wu=wu_t, wd=wd_t,
            toko=np.array([[half * 1024]], np.int32),
        ))

    if _NC_CACHE is None:
        _NC_CACHE = build()
    res = run_bass_kernel_spmd(_NC_CACHE, in_maps, core_ids=list(range(8)))

    out = np.empty((4, 2048, 4096), np.float32)
    for c in range(8):
        b, half = c // 2, c % 2
        out[b, half * 1024:(half + 1) * 1024, :] = res.results[c]["outfm"].T
    return out


# version 13 (final)
# speedup vs baseline: 1.1637x; 1.0107x over previous
import numpy as np
from contextlib import ExitStack

import concourse.bass as bass
import concourse.mybir as mybir
from concourse.bass_utils import run_bass_kernel_spmd

F32 = mybir.dt.float32
BF16 = mybir.dt.bfloat16
I32 = mybir.dt.int32
AF = mybir.ActivationFunctionType
ALU = mybir.AluOpType

P = 128
T = 512
NCH = 4
DTl = 16
NS = 16
SPLIT_J = 2
NKT = 32
NIT = 86
NJT = 32
EPS = 1e-5
GP_J = DTl - SPLIT_J

SEMS = [
    "LD", "LDG", "SETS", "SETV",
    "XBF", "XH", "XP", "DTIN", "BCV", "DTM", "SP", "YI",
    "BC", "CSB", "DA", "SC", "HC", "VCH", "R1", "CC", "GB", "YA", "HS", "SETG",
    "CP", "HCB", "XPH", "BCD", "PP", "PPD",
    "LNL", "SQ", "S12", "STV", "STS", "MBC", "ZV", "ZBF", "ZSC",
    "WG", "WU", "MMG", "MMU", "SIL", "GUV", "GUS",
    "GUL", "WD", "MMD", "ZJL", "R2V", "SQ2", "S34", "R2S",
    "ST2V", "ST2S", "M2BC", "R2L", "OV", "OUT",
]


def build():
    nc = bass.Bass("TRN2", target_bir_lowering=False)

    xfm = nc.dram_tensor("xfm", [4096, 2048], F32, kind="ExternalInput").ap()
    xhh = nc.dram_tensor("xhh", [2048, 2048], F32, kind="ExternalInput").ap()
    xpwt = nc.dram_tensor("xpwt", [128, 5120], F32, kind="ExternalInput").ap()
    dtwt = nc.dram_tensor("dtwt", [128, 2048], F32, kind="ExternalInput").ap()
    dtbi = nc.dram_tensor("dtbi", [128, 16], F32, kind="ExternalInput").ap()
    alog = nc.dram_tensor("alog", [128, 256], F32, kind="ExternalInput").ap()
    dvec = nc.dram_tensor("dvec", [128, 16], F32, kind="ExternalInput").ap()
    ones1 = nc.dram_tensor("ones1", [128, 1], F32, kind="ExternalInput").ap()
    onebc = nc.dram_tensor("onebc", [1, 128], F32, kind="ExternalInput").ap()
    wg = nc.dram_tensor("wg", [NIT, 128, 4096], F32, kind="ExternalInput").ap()
    wu = nc.dram_tensor("wu", [NIT, 128, 4096], F32, kind="ExternalInput").ap()
    wd = nc.dram_tensor("wd", [NJT, 128, 11008], F32, kind="ExternalInput").ap()
    toko = nc.dram_tensor("toko", [1, 1], I32, kind="ExternalInput").ap()
    outfm = nc.dram_tensor("outfm", [4096, 1024], F32, kind="ExternalOutput").ap()

    bcd = nc.dram_tensor("bcd", [32, 512], F32).ap()
    wgbf = nc.dram_tensor("wgbf", [NIT, 128, 4096], BF16).ap()
    wubf = nc.dram_tensor("wubf", [NIT, 128, 4096], BF16).ap()
    wdbf = nc.dram_tensor("wdbf", [NJT, 128, 11008], BF16).ap()
    cc_in = nc.dram_tensor("cc_in", [2048, 2048], F32)
    cc_out = nc.dram_tensor("cc_out", [4096, 2048], F32)
    gusc = nc.dram_tensor("gusc", [NIT, 128, 1024], BF16).ap()
    zsc = nc.dram_tensor("zsc", [4096, 1024], F32).ap()
    r2sc = nc.dram_tensor("r2sc", [4096, 1024], F32).ap()

    xv = xfm.rearrange("(kt p) t -> p kt t", p=P)
    xhv = xhh.rearrange("(j p) t -> p j t", p=P)
    ccin_v = cc_in.ap().rearrange("(j p) t -> p j t", p=P)
    ccout_v = cc_out.ap().rearrange("(kt p) t -> p kt t", p=P)
    gusc_v = gusc.rearrange("it p t -> p it t")

    def sidx(c, n, j):
        return c * 256 + n * 16 + j + 1

    def pidx(c, n, j):
        return c * (16 * GP_J) + n * GP_J + (j - SPLIT_J) + 1

    with ExitStack() as PS:
        E = PS.enter_context
        sem = {name: E(nc.semaphore(name)) for name in SEMS}
        block = E(nc.Block())

        xpw_bf = E(nc.sbuf_tensor("xpw_bf", [P, 5120], BF16))
        dtw_bf = E(nc.sbuf_tensor("dtw_bf", [P, 2048], BF16))
        dtb_sb = E(nc.sbuf_tensor("dtb_sb", [P, 16], F32))
        A_sb = E(nc.sbuf_tensor("A_sb", [P, 256], F32))
        d1_sb = E(nc.sbuf_tensor("d1_sb", [P, 16], F32))
        ones_sb = E(nc.sbuf_tensor("ones_sb", [P, 1], F32))
        onebc_sb = E(nc.sbuf_tensor("onebc_sb", [1, 128], F32))
        h_state = E(nc.sbuf_tensor("h_state", [P, 256], F32))
        cst_sb = E(nc.sbuf_tensor("cst_sb", [P, 4], F32))

        @block.sync
        def _(sync):
            sync.dma_start(out=dtb_sb[:], in_=dtbi).then_inc(sem["LD"], 16)
            sync.dma_start(out=A_sb[:], in_=alog).then_inc(sem["LD"], 16)
            sync.dma_start(out=d1_sb[:], in_=dvec).then_inc(sem["LD"], 16)
            sync.dma_start(out=ones_sb[:], in_=ones1).then_inc(sem["LD"], 16)
            sync.dma_start(out=onebc_sb[:], in_=onebc).then_inc(sem["LD"], 16)

        @block.gpsimd
        def _(g):
            g.dma_start(out=xpw_bf[:], in_=xpwt).then_inc(sem["LDG"], 16)
            g.dma_start(out=dtw_bf[:], in_=dtwt).then_inc(sem["LDG"], 16)


        @block.scalar
        def _(s):
            s.wait_ge(sem["LD"], 80)
            s.activation(out=A_sb[:], in_=A_sb[:], func=AF.Exp).then_inc(sem["SETS"], 1)

        @block.vector
        def _(v):
            v.wait_ge(sem["SETS"], 1)
            v.tensor_scalar_mul(out=A_sb[:], in0=A_sb[:], scalar1=-1.0)
            v.tensor_scalar_add(out=d1_sb[:], in0=d1_sb[:], scalar1=1.0)
            v.memset(cst_sb[:, 0:1], 1.0)
            v.memset(cst_sb[:, 1:2], EPS)
            v.memset(cst_sb[:, 2:3], -0.5)
            v.memset(h_state[:], 0.0).then_inc(sem["SETV"], 1)

        with ExitStack() as SA:
            Ea = SA.enter_context
            xbf_sb = Ea(nc.sbuf_tensor("xbf_sb", [P, 8 * T], BF16))
            xh_sb = Ea(nc.sbuf_tensor("xh_sb", [P, DTl * T], F32))
            dt_sb = Ea(nc.sbuf_tensor("dt_sb", [P, DTl * T], F32))
            y_sb = Ea(nc.sbuf_tensor("y_sb", [P, DTl * T], F32))
            dtin_bf = Ea(nc.sbuf_tensor("dtin_bf", [P, T], BF16))
            bc_sb = Ea(nc.sbuf_tensor("bc_sb", [32, T], F32))
            e1_sb = Ea(nc.sbuf_tensor("e1_sb", [P, T], F32))
            B_sb = [Ea(nc.sbuf_tensor(f"B_sb{k}", [P, T], F32)) for k in range(2)]
            C_sb = [Ea(nc.sbuf_tensor(f"C_sb{k}", [P, T], F32)) for k in range(2)]
            dA_sb = [Ea(nc.sbuf_tensor(f"dA_sb{k}", [P, T + 1], F32)) for k in range(4)]
            dBxB = [Ea(nc.sbuf_tensor(f"dBxB{k}", [P, 4 * (T + 1)], F32)) for k in range(3)]
            hrB = [Ea(nc.sbuf_tensor(f"hrB{k}", [P, 4 * (T + 1)], F32)) for k in range(3)]
            hcB = [Ea(nc.sbuf_tensor(f"hcB{k}", [P, 4 * T], F32)) for k in range(2)]
            ps_xp1 = Ea(nc.psum_tensor("ps_xp1", [P, T], F32))
            ps_xp2 = Ea(nc.psum_tensor("ps_xp2", [32, T], F32))
            ps_dt = [Ea(nc.psum_tensor(f"ps_dt{k}", [P, T], F32)) for k in range(2)]

            xbf_v = xbf_sb[:].rearrange("p (kt t) -> p kt t", t=T)
            xh_v = xh_sb[:].rearrange("p (j t) -> p j t", t=T)
            y_v = y_sb[:].rearrange("p (j t) -> p j t", t=T)

            def gbi(c, n, jq):
                return c * 64 + n * 4 + jq + 1

            @block.gpsimd
            def _(g):
                g.memset(dA_sb[0][:], 0.0)
                g.memset(dA_sb[1][:], 0.0)
                g.memset(dA_sb[2][:], 0.0)
                g.memset(dA_sb[3][:], 0.0)
                g.memset(dBxB[0][:], 0.0)
                g.memset(dBxB[2][:], 0.0)
                g.memset(dBxB[1][:], 0.0).then_inc(sem["SETG"], 1)

            for c in range(NCH):
                t0 = c * T

                @block.sync
                def _(sync, c=c, t0=t0):
                    if c > 0:
                        sync.wait_ge(sem["GB"], c * 64)
                    for q in range(2):
                        sync.dma_start(
                            out=xh_v[:, q * 8:(q + 1) * 8, :],
                            in_=xhv[:, q * 8:(q + 1) * 8, t0:t0 + T],
                        ).then_inc(sem["XH"], 16)
                    sync.wait_ge(sem["BCV"], c + 1)
                    sync.dma_start(out=bcd, in_=bc_sb[:]).then_inc(sem["BCD"], 16)
                    sync.wait_ge(sem["BCD"], (c + 1) * 16)
                    for n in range(NS):
                        N = c * 16 + n
                        if N >= 2:
                            sync.wait_ge(sem["GB"], (N - 1) * 4)
                        sync.dma_start(out=B_sb[n % 2][:],
                                       in_=bcd[n:n + 1, :].broadcast_to([P, T])).then_inc(sem["CSB"], 16)
                        if N >= 2:
                            sync.wait_ge(sem["HCB"], (N - 1) * 4)
                        sync.dma_start(out=C_sb[n % 2][:],
                                       in_=bcd[16 + n:17 + n, :].broadcast_to([P, T])).then_inc(sem["CSB"], 16)
                    sync.wait_ge(sem["YA"], (c + 1) * 64)
                    for q in range(2):
                        sync.dma_start(
                            out=ccin_v[:, q * 8:(q + 1) * 8, t0:t0 + T],
                            in_=y_v[:, q * 8:(q + 1) * 8, :],
                        ).then_inc(sem["R1"], 16)

                @block.gpsimd
                def _(g, c=c, t0=t0):
                    for grp in range(4):
                        if grp == 0:
                            if c > 0:
                                g.wait_ge(sem["XP"], c)
                        else:
                            g.wait_ge(sem["XPH"], c * 3 + grp)
                        g.dma_start(
                            out=xbf_v[:, 0:8, :],
                            in_=xv[:, grp * 8:(grp + 1) * 8, t0:t0 + T],
                        ).then_inc(sem["XBF"], 16)
                    ppb = [0, 8, 15, 22, 29, 36, 43, 51, 58, 65, 72, 79, 86]

                    def emit_dbx(kb):
                        n, jq = kb // 4, kb % 4
                        j0 = jq * 4
                        gbg = gbi(c, n, jq)
                        if jq == 0:
                            g.wait_ge(sem["CSB"], (c * 16 + n + 1) * 32)
                        if gbg >= 4:
                            g.wait_ge(sem["SC"], (gbg - 3) * 4)
                        g.wait_ge(sem["YI"], c * 16 + j0 + 4)
                        g.tensor_tensor(
                            out=dBxB[(gbg - 1) % 3][:].rearrange("p (q t) -> p q t", t=T + 1)[:, :, 1:T + 1],
                            in0=xh_sb[:, j0 * T:(j0 + 4) * T].rearrange("p (q t) -> p q t", t=T),
                            in1=B_sb[n % 2][:].unsqueeze(1).broadcast_to([P, 4, T]),
                            op=ALU.mult).then_inc(sem["GB"], 1)

                    def emit_hc(kb):
                        gbp = gbi(c, kb // 4, kb % 4)
                        npp = kb // 4
                        g.wait_ge(sem["SC"], gbp * 4)
                        if gbp >= 3:
                            g.wait_ge(sem["YA"], gbp - 2)
                        g.tensor_tensor(
                            out=hcB[(gbp - 1) % 2][:].rearrange("p (q t) -> p q t", t=T),
                            in0=hrB[(gbp - 1) % 3][:].rearrange("p (q t) -> p q t", t=T + 1)[:, :, 1:T + 1],
                            in1=C_sb[npp % 2][:].unsqueeze(1).broadcast_to([P, 4, T]),
                            op=ALU.mult).then_inc(sem["HCB"], 1)

                    emit_dbx(0)
                    emit_dbx(1)
                    for kb in range(64):
                        if kb in (16, 32, 48):
                            p = c * 3 + kb // 16 - 1
                            i0, i1 = ppb[p], ppb[p + 1]
                            g.dma_start(out=wgbf[i0:i1], in_=wg[i0:i1]).then_inc(sem["PP"], 16)
                            g.dma_start(out=wubf[i0:i1], in_=wu[i0:i1]).then_inc(sem["PP"], 16)
                        if c < 2 and kb in (8, 24, 40, 56):
                            p = c * 4 + kb // 16
                            g.dma_start(out=wdbf[p * 4:(p + 1) * 4], in_=wd[p * 4:(p + 1) * 4]).then_inc(sem["PPD"], 16)
                        if kb + 2 < 64:
                            emit_dbx(kb + 2)
                        emit_hc(kb)

                @block.tensor
                def _(tn, c=c):
                    if c == 0:
                        tn.wait_ge(sem["LDG"], 32)
                    else:
                        tn.wait_ge(sem["DTIN"], c)
                        tn.wait_ge(sem["BCV"], c)
                    for grp in range(4):
                        tn.wait_ge(sem["XBF"], c * 64 + (grp + 1) * 16)
                        for kt in range(8):
                            kg = grp * 8 + kt
                            st = kg == 0
                            sp = kg == 31
                            tn.matmul(ps_xp1[:], xpw_bf[:, kg * 160:kg * 160 + 128],
                                      xbf_sb[:, kt * T:(kt + 1) * T], start=st, stop=sp)
                            ins = tn.matmul(ps_xp2[:], xpw_bf[:, kg * 160 + 128:kg * 160 + 160],
                                            xbf_sb[:, kt * T:(kt + 1) * T], start=st, stop=sp)
                        if grp < 3:
                            ins.then_inc(sem["XPH"], 1)
                        else:
                            ins.then_inc(sem["XP"], 1)
                    tn.wait_ge(sem["DTIN"], c + 1)
                    for j in range(DTl):
                        if c * 16 + j >= 2:
                            tn.wait_ge(sem["SP"], c * 16 + j - 1)
                        tn.matmul(ps_dt[j % 2][:], dtw_bf[:, j * 128:(j + 1) * 128],
                                  dtin_bf[:], start=True, stop=True).then_inc(sem["DTM"], 1)

                @block.scalar
                def _(s, c=c):
                    if c == 0:
                        s.wait_ge(sem["SETV"], 1)
                        s.wait_ge(sem["SETG"], 1)
                    s.wait_ge(sem["XP"], c + 1)
                    s.activation(out=dtin_bf[:], in_=ps_xp1[:], func=AF.Copy).then_inc(sem["DTIN"], 1)
                    for j in range(DTl):
                        s.wait_ge(sem["DTM"], c * 16 + j + 1)
                        if c >= 1:
                            s.wait_ge(sem["YI"], (c - 1) * 16 + j + 1)
                        s.activation(out=e1_sb[:], in_=ps_dt[j % 2][:], func=AF.Exp,
                                     bias=dtb_sb[:, j:j + 1])
                        s.activation(out=dt_sb[:, j * T:(j + 1) * T], in_=e1_sb[:],
                                     func=AF.Ln, bias=cst_sb[:, 0:1]).then_inc(sem["SP"], 1)
                    for n in range(NS):
                        N = c * 16 + n
                        for j in range(DTl):
                            I = sidx(c, n, j)
                            if c >= 1 and j % 4 == 0:
                                jq = j // 4
                                gbg = gbi(c, n, jq)
                                if gbg >= 4:
                                    s.wait_ge(sem["SC"], (gbg - 3) * 4)
                                s.activation(
                                    out=dBxB[(gbg - 1) % 3][:].rearrange("p (q t) -> p q t", t=T + 1)[:, :, 0:1].squeeze(2),
                                    in_=h_state[:, (jq * 64 + n):(jq * 64 + n + 49):16],
                                    func=AF.Copy).then_inc(sem["CP"], 1)
                            if I >= 5:
                                s.wait_ge(sem["SC"], I - 4)
                            s.activation(out=dA_sb[(I - 1) % 4][:, 1:T + 1],
                                         in_=dt_sb[:, j * T:(j + 1) * T], func=AF.Exp,
                                         scale=A_sb[:, (j * 16 + n):(j * 16 + n) + 1]
                                         ).then_inc(sem["DA"], 1)
                            if I >= 3:
                                cp, kp = divmod(I - 3, 256)
                                if cp <= 2:
                                    npp, jp = kp // 16, kp % 16
                                    seg = kp % 4 if False else ((I - 3) % 4)
                                    s.wait_ge(sem["SC"], I - 2)
                                    s.activation(out=h_state[:, (jp * 16 + npp):(jp * 16 + npp) + 1],
                                                 in_=hrB[((I - 3) // 4) % 3][:, seg * (T + 1) + T: seg * (T + 1) + T + 1],
                                                 func=AF.Copy).then_inc(sem["HS"], 1)

                @block.vector
                def _(v, c=c):
                    v.wait_ge(sem["XP"], c + 1)
                    if c > 0:
                        v.wait_ge(sem["BCD"], c * 16)
                    v.tensor_copy(out=bc_sb[:], in_=ps_xp2[:]).then_inc(sem["BCV"], 1)
                    v.wait_ge(sem["XH"], (c + 1) * 32)
                    if c > 0:
                        v.wait_ge(sem["R1"], c * 32)
                    for j in range(DTl):
                        v.tensor_scalar(out=y_sb[:, j * T:(j + 1) * T],
                                        in0=xh_sb[:, j * T:(j + 1) * T],
                                        scalar1=d1_sb[:, j:j + 1], scalar2=None,
                                        op0=ALU.mult)
                        v.wait_ge(sem["SP"], c * 16 + j + 1)
                        v.tensor_tensor(out=xh_sb[:, j * T:(j + 1) * T],
                                        in0=dt_sb[:, j * T:(j + 1) * T],
                                        in1=xh_sb[:, j * T:(j + 1) * T],
                                        op=ALU.mult).then_inc(sem["YI"], 1)
                    for kb in range(64):
                        n, jq = kb // 4, kb % 4
                        gbg = gbi(c, n, jq)
                        for q in range(4):
                            k = kb * 4 + q
                            I = c * 256 + k + 1
                            v.wait_ge(sem["DA"], I)
                            v.wait_ge(sem["GB"], gbg)
                            if c >= 1:
                                v.wait_ge(sem["CP"], (gbg - 64))
                            if gbg >= 4:
                                v.wait_ge(sem["HCB"], gbg - 3)
                            if I >= 13:
                                v.wait_ge(sem["HS"], min(I - 12, 3 * 256))
                            seg = (I - 1) % 4
                            v.tensor_tensor_scan(
                                out=hrB[(gbg - 1) % 3][:, seg * (T + 1):(seg + 1) * (T + 1)],
                                data0=dA_sb[(I - 1) % 4][:],
                                data1=dBxB[(gbg - 1) % 3][:, seg * (T + 1):(seg + 1) * (T + 1)],
                                initial=0.0,
                                op0=ALU.mult, op1=ALU.add).then_inc(sem["SC"], 1)
                        if kb >= 1:
                            gbp = gbg - 1
                            j0p = ((kb - 1) % 4) * 4
                            v.wait_ge(sem["HCB"], gbp)
                            v.tensor_tensor(out=y_sb[:, j0p * T:(j0p + 4) * T],
                                            in0=y_sb[:, j0p * T:(j0p + 4) * T],
                                            in1=hcB[(gbp - 1) % 2][:],
                                            op=ALU.add).then_inc(sem["YA"], 1)
                    gbp = gbi(c, 15, 3)
                    v.wait_ge(sem["HCB"], gbp)
                    v.tensor_tensor(out=y_sb[:, 12 * T:16 * T],
                                    in0=y_sb[:, 12 * T:16 * T],
                                    in1=hcB[(gbp - 1) % 2][:],
                                    op=ALU.add).then_inc(sem["YA"], 1)

        @block.gpsimd
        def _(g):
            g.wait_ge(sem["R1"], NCH * 32)
            g.collective_compute(
                "AllGather", ALU.bypass,
                replica_groups=[[0, 1], [2, 3], [4, 5], [6, 7]],
                ins=[cc_in.ap().opt()], outs=[cc_out.ap().opt()],
            ).then_inc(sem["CC"], 1)

        with ExitStack() as SC:
            Ec = SC.enter_context
            z_bf = Ec(nc.sbuf_tensor("z_bf", [P, NKT * 1024], BF16))

            with ExitStack() as SC1:
                E1 = SC1.enter_context
                r1_sb = E1(nc.sbuf_tensor("r1_sb", [P, NKT * T], F32))
                sq_sb = [E1(nc.sbuf_tensor(f"sq_sb{k}", [P, T], F32)) for k in range(2)]
                z32_sb = [E1(nc.sbuf_tensor(f"z32_sb{k}", [P, T], F32)) for k in range(2)]
                t1_sb = E1(nc.sbuf_tensor("t1_sb", [P, T], F32))
                stat4 = E1(nc.sbuf_tensor("stat4", [1, 4 * T], F32))
                m_sb = stat4[0:1, 0:T]
                s2m_sb = stat4[0:1, T:2 * T]
                var_sb = stat4[0:1, 2 * T:3 * T]
                rstd_sb = stat4[0:1, 3 * T:4 * T]
                ps_s1 = E1(nc.psum_tensor("ps_s1", [1, T], F32))
                ps_s2 = E1(nc.psum_tensor("ps_s2", [1, T], F32))
                ps_mb = E1(nc.psum_tensor("ps_mb", [P, T], F32))
                ps_rb = E1(nc.psum_tensor("ps_rb", [P, T], F32))

                r1_v = r1_sb[:].rearrange("p (kt t) -> p kt t", t=T)

                for tc in range(2):
                    @block.gpsimd
                    def _(g, tc=tc):
                        g.wait_ge(sem["CC"], 1)
                        if tc > 0:
                            g.wait_ge(sem["ZV"], tc * 32)
                        with g.register(f"rtok{tc}") as rtok:
                            g.reg_load(rtok, toko[0:1, 0:1])
                            voff = g.snap(rtok, min_val=0, max_val=1024)
                            for q in range(4):
                                g.dma_start(
                                    out=r1_v[:, q * 8:(q + 1) * 8, :],
                                    in_=ccout_v[:, q * 8:(q + 1) * 8, bass.ds(voff + tc * T, T)],
                                ).then_inc(sem["LNL"], 16)

                    @block.scalar
                    def _(s, tc=tc):
                        for kt in range(NKT):
                            k = tc * 32 + kt
                            s.wait_ge(sem["LNL"], tc * 64 + (kt // 8 + 1) * 16)
                            if k >= 2:
                                s.wait_ge(sem["S12"], k - 1)
                            s.activation(out=sq_sb[kt % 2][:],
                                         in_=r1_sb[:, kt * T:(kt + 1) * T],
                                         func=AF.Square).then_inc(sem["SQ"], 1)

                    @block.tensor
                    def _(tn, tc=tc):
                        if tc > 0:
                            tn.wait_ge(sem["STV"], tc)
                        for kt in range(NKT):
                            k = tc * 32 + kt
                            tn.wait_ge(sem["LNL"], tc * 64 + (kt // 8 + 1) * 16)
                            tn.matmul(ps_s1[:], ones_sb[:], r1_sb[:, kt * T:(kt + 1) * T],
                                      start=kt == 0, stop=kt == NKT - 1)
                            tn.wait_ge(sem["SQ"], k + 1)
                            tn.matmul(ps_s2[:], ones_sb[:], sq_sb[kt % 2][:],
                                      start=kt == 0, stop=kt == NKT - 1).then_inc(sem["S12"], 1)
                        tn.wait_ge(sem["STS"], tc + 1)
                        if tc > 0:
                            tn.wait_ge(sem["ZV"], tc * 32)
                        tn.matmul(ps_mb[:], onebc_sb[:], m_sb, start=True, stop=True)
                        tn.matmul(ps_rb[:], onebc_sb[:], rstd_sb, start=True, stop=True).then_inc(sem["MBC"], 1)

                    @block.vector
                    def _(v, tc=tc):
                        v.wait_ge(sem["S12"], (tc + 1) * 32)
                        v.tensor_scalar_mul(out=m_sb, in0=ps_s1[:], scalar1=1.0 / 4096)
                        v.tensor_scalar_mul(out=s2m_sb, in0=ps_s2[:], scalar1=1.0 / 4096)
                        v.tensor_tensor(out=var_sb, in0=m_sb, in1=m_sb, op=ALU.mult)
                        v.tensor_tensor(out=var_sb, in0=s2m_sb, in1=var_sb,
                                        op=ALU.subtract).then_inc(sem["STV"], 1)
                        for kt in range(NKT):
                            k = tc * 32 + kt
                            v.wait_ge(sem["MBC"], tc + 1)
                            if k >= 2:
                                v.wait_ge(sem["ZBF"], k - 1)
                                v.wait_ge(sem["ZSC"], (k - 1) * 16)
                            v.tensor_tensor(out=t1_sb[:], in0=r1_sb[:, kt * T:(kt + 1) * T],
                                            in1=ps_mb[:], op=ALU.subtract)
                            v.tensor_tensor(out=z32_sb[kt % 2][:], in0=t1_sb[:],
                                            in1=ps_rb[:], op=ALU.mult).then_inc(sem["ZV"], 1)

                    @block.scalar
                    def _(s, tc=tc):
                        s.wait_ge(sem["STV"], tc + 1)
                        s.activation(out=var_sb, in_=var_sb, func=AF.Ln,
                                     bias=cst_sb[0:1, 1:2])
                        s.activation(out=rstd_sb, in_=var_sb, func=AF.Exp,
                                     scale=cst_sb[0:1, 2:3]).then_inc(sem["STS"], 1)
                        for kt in range(NKT):
                            k = tc * 32 + kt
                            s.wait_ge(sem["ZV"], k + 1)
                            s.activation(out=z_bf[:, kt * 1024 + tc * T: kt * 1024 + tc * T + T],
                                         in_=z32_sb[kt % 2][:], func=AF.Copy).then_inc(sem["ZBF"], 1)

                    @block.sync
                    def _(sync, tc=tc):
                        for kt in range(NKT):
                            k = tc * 32 + kt
                            sync.wait_ge(sem["ZV"], k + 1)
                            sync.dma_start(
                                out=zsc[kt * 128:(kt + 1) * 128, tc * T:(tc + 1) * T],
                                in_=z32_sb[kt % 2][:]).then_inc(sem["ZSC"], 16)

            with ExitStack() as SC2:
                E2 = SC2.enter_context
                wg_bf = [E2(nc.sbuf_tensor(f"wg_bf{k}", [P, 4096], BF16)) for k in range(2)]
                wu_bf = [E2(nc.sbuf_tensor(f"wu_bf{k}", [P, 4096], BF16)) for k in range(2)]
                sg_sb = [E2(nc.sbuf_tensor(f"sg_sb{k}", [P, T], F32)) for k in range(2)]
                gu_sb = [E2(nc.sbuf_tensor(f"gu_sb{k}", [P, 1024], BF16)) for k in range(2)]
                ps_g = [E2(nc.psum_tensor(f"ps_g{k}", [P, T], F32)) for k in range(2)]
                ps_u = [E2(nc.psum_tensor(f"ps_u{k}", [P, T], F32)) for k in range(2)]

                @block.scalar
                def _(s):
                    s.wait_ge(sem["PP"], 24 * 16)
                    s.wait_ge(sem["ZBF"], 64)
                    s.wait_ge(sem["ZSC"], 64 * 16)
                    for i in range(NIT):
                        if i >= 2:
                            s.wait_ge(sem["MMG"], 2 * (i - 1))
                        s.dma_start(out=wg_bf[i % 2][:], in_=wgbf[i]).then_inc(sem["WG"], 16)
                        if i >= 2:
                            s.wait_ge(sem["MMU"], 2 * (i - 1))
                        s.dma_start(out=wu_bf[i % 2][:], in_=wubf[i]).then_inc(sem["WU"], 16)
                        if i >= 1:
                            for tc in range(2):
                                q = 2 * (i - 1) + tc
                                s.wait_ge(sem["MMG"], q + 1)
                                if q >= 2:
                                    s.wait_ge(sem["GUV"], q - 1)
                                s.activation(out=sg_sb[tc % 2][:], in_=ps_g[tc][:],
                                             func=AF.Silu).then_inc(sem["SIL"], 1)
                    for tc in range(2):
                        q = 2 * (NIT - 1) + tc
                        s.wait_ge(sem["MMG"], q + 1)
                        s.wait_ge(sem["GUV"], q - 1)
                        s.activation(out=sg_sb[tc % 2][:], in_=ps_g[tc][:],
                                     func=AF.Silu).then_inc(sem["SIL"], 1)

                @block.tensor
                def _(tn):
                    tn.wait_ge(sem["ZBF"], 64)
                    for i in range(NIT):
                        tn.wait_ge(sem["WG"], (i + 1) * 16)
                        for tc in range(2):
                            if 2 * i + tc >= 2:
                                tn.wait_ge(sem["SIL"], 2 * i + tc - 1)
                            for kt in range(NKT):
                                ins = tn.matmul(
                                    ps_g[tc][:], wg_bf[i % 2][:, kt * 128:(kt + 1) * 128],
                                    z_bf[:, kt * 1024 + tc * T: kt * 1024 + tc * T + T],
                                    start=kt == 0, stop=kt == NKT - 1)
                            ins.then_inc(sem["MMG"], 1)
                        tn.wait_ge(sem["WU"], (i + 1) * 16)
                        for tc in range(2):
                            if 2 * i + tc >= 2:
                                tn.wait_ge(sem["GUV"], 2 * i + tc - 1)
                            for kt in range(NKT):
                                ins = tn.matmul(
                                    ps_u[tc][:], wu_bf[i % 2][:, kt * 128:(kt + 1) * 128],
                                    z_bf[:, kt * 1024 + tc * T: kt * 1024 + tc * T + T],
                                    start=kt == 0, stop=kt == NKT - 1)
                            ins.then_inc(sem["MMU"], 1)

                @block.vector
                def _(v):
                    for i in range(NIT):
                        for tc in range(2):
                            q = 2 * i + tc
                            v.wait_ge(sem["SIL"], q + 1)
                            v.wait_ge(sem["MMU"], q + 1)
                            if i >= 2:
                                v.wait_ge(sem["GUS"], (i - 1) * 16)
                            v.tensor_tensor(out=gu_sb[i % 2][:, tc * T:(tc + 1) * T],
                                            in0=sg_sb[tc % 2][:], in1=ps_u[tc][:],
                                            op=ALU.mult).then_inc(sem["GUV"], 1)

                @block.sync
                def _(sync):
                    for i in range(NIT):
                        sync.wait_ge(sem["GUV"], 2 * i + 2)
                        sync.dma_start(out=gusc[i], in_=gu_sb[i % 2][:]).then_inc(sem["GUS"], 16)

        with ExitStack() as SD:
            Ed = SD.enter_context
            gu2_sb = Ed(nc.sbuf_tensor("gu2_sb", [P, NIT * T], BF16))
            wd_bf = [Ed(nc.sbuf_tensor(f"wd_bf{k}", [P, NIT * 128], BF16)) for k in range(2)]
            zj_sb = [Ed(nc.sbuf_tensor(f"zj_sb{k}", [P, T], F32)) for k in range(4)]
            r2_sb = [Ed(nc.sbuf_tensor(f"r2_sb{k}", [P, T], F32)) for k in range(2)]
            sq2_sb = [Ed(nc.sbuf_tensor(f"sq2_sb{k}", [P, T], F32)) for k in range(2)]
            rl_sb = [Ed(nc.sbuf_tensor(f"rl_sb{k}", [P, T], F32)) for k in range(2)]
            o32_sb = [Ed(nc.sbuf_tensor(f"o32_sb{k}", [P, T], F32)) for k in range(2)]
            t2_sb = Ed(nc.sbuf_tensor("t2_sb", [P, T], F32))
            stat4b = Ed(nc.sbuf_tensor("stat4b", [1, 4 * T], F32))
            m_sb = stat4b[0:1, 0:T]
            s2m_sb = stat4b[0:1, T:2 * T]
            var_sb = stat4b[0:1, 2 * T:3 * T]
            rstd_sb = stat4b[0:1, 3 * T:4 * T]
            ps_d = [Ed(nc.psum_tensor(f"ps_d{k}", [P, T], F32)) for k in range(2)]
            ps_s3 = Ed(nc.psum_tensor("ps_s3", [1, T], F32))
            ps_s4 = Ed(nc.psum_tensor("ps_s4", [1, T], F32))
            ps_m2 = Ed(nc.psum_tensor("ps_m2", [P, T], F32))
            ps_r2 = Ed(nc.psum_tensor("ps_r2", [P, T], F32))

            gu2_v = gu2_sb[:].rearrange("p (it t) -> p it t", t=T)

            for tc in range(2):
                @block.sync
                def _(sync, tc=tc):
                    if tc == 0:
                        sync.wait_ge(sem["GUS"], NIT * 16)
                        sync.wait_ge(sem["ZSC"], 64 * 16)
                    else:
                        sync.wait_ge(sem["MMD"], tc * 32)
                    for q in range(11):
                        i0 = q * 8
                        i1 = min(i0 + 8, NIT)
                        sync.dma_start(
                            out=gu2_v[:, i0:i1, :],
                            in_=gusc_v[:, i0:i1, tc * T:(tc + 1) * T],
                        ).then_inc(sem["GUL"], 16)
                    for j in range(NJT):
                        JD = tc * 32 + j
                        sync.wait_ge(sem["R2V"], JD + 1)
                        sync.dma_start(out=r2sc[j * 128:(j + 1) * 128, tc * T:(tc + 1) * T],
                                       in_=r2_sb[j % 2][:]).then_inc(sem["R2S"], 16)
                    for j in range(NJT):
                        JD = tc * 32 + j
                        if JD >= 2:
                            sync.wait_ge(sem["OV"], JD - 1)
                        sync.dma_start(out=rl_sb[j % 2][:],
                                       in_=r2sc[j * 128:(j + 1) * 128, tc * T:(tc + 1) * T]
                                       ).then_inc(sem["R2L"], 16)
                        sync.wait_ge(sem["OV"], JD + 1)
                        sync.dma_start(out=outfm[j * 128:(j + 1) * 128, tc * T:(tc + 1) * T],
                                       in_=o32_sb[j % 2][:]).then_inc(sem["OUT"], 16)

                @block.gpsimd
                def _(g, tc=tc):
                    if tc == 0:
                        g.wait_ge(sem["GUS"], NIT * 16)
                        g.wait_ge(sem["PPD"], 128)
                    for j in range(NJT):
                        JD = tc * 32 + j
                        if JD >= 2:
                            g.wait_ge(sem["MMD"], JD - 1)
                        g.dma_start(out=wd_bf[j % 2][:], in_=wdbf[j]).then_inc(sem["WD"], 16)

                @block.tensor
                def _(tn, tc=tc):
                    tn.wait_ge(sem["GUL"], tc * 176 + 176)
                    if tc > 0:
                        tn.wait_ge(sem["ST2V"], tc)
                    for j in range(NJT):
                        JD = tc * 32 + j
                        tn.wait_ge(sem["WD"], (JD + 1) * 16)
                        if JD >= 2:
                            tn.wait_ge(sem["R2V"], JD - 1)
                        for i in range(NIT):
                            ins = tn.matmul(ps_d[j % 2][:],
                                            wd_bf[j % 2][:, i * 128:(i + 1) * 128],
                                            gu2_sb[:, i * T:(i + 1) * T],
                                            start=i == 0, stop=i == NIT - 1)
                        ins.then_inc(sem["MMD"], 1)
                        if j >= 1:
                            jp = j - 1
                            JP = tc * 32 + jp
                            tn.wait_ge(sem["R2V"], JP + 1)
                            tn.matmul(ps_s3[:], ones_sb[:], r2_sb[jp % 2][:],
                                      start=jp == 0, stop=False)
                            tn.wait_ge(sem["SQ2"], JP + 1)
                            tn.matmul(ps_s4[:], ones_sb[:], sq2_sb[jp % 2][:],
                                      start=jp == 0, stop=False).then_inc(sem["S34"], 1)
                    jp = NJT - 1
                    JP = tc * 32 + jp
                    tn.wait_ge(sem["R2V"], JP + 1)
                    tn.matmul(ps_s3[:], ones_sb[:], r2_sb[jp % 2][:],
                              start=False, stop=True)
                    tn.wait_ge(sem["SQ2"], JP + 1)
                    tn.matmul(ps_s4[:], ones_sb[:], sq2_sb[jp % 2][:],
                              start=False, stop=True).then_inc(sem["S34"], 1)
                    tn.wait_ge(sem["ST2S"], tc + 1)
                    if tc > 0:
                        tn.wait_ge(sem["OV"], tc * 32)
                    tn.matmul(ps_m2[:], onebc_sb[:], m_sb, start=True, stop=True)
                    tn.matmul(ps_r2[:], onebc_sb[:], rstd_sb, start=True, stop=True).then_inc(sem["M2BC"], 1)

                @block.vector
                def _(v, tc=tc):
                    for j in range(NJT):
                        JD = tc * 32 + j
                        v.wait_ge(sem["MMD"], JD + 1)
                        v.wait_ge(sem["ZJL"], (JD + 1) * 16)
                        if JD >= 2:
                            v.wait_ge(sem["SQ2"], JD - 1)
                            v.wait_ge(sem["S34"], JD - 1)
                            v.wait_ge(sem["R2S"], (JD - 1) * 16)
                        v.tensor_tensor(out=r2_sb[j % 2][:], in0=ps_d[j % 2][:],
                                        in1=zj_sb[j % 4][:], op=ALU.add).then_inc(sem["R2V"], 1)
                    v.wait_ge(sem["S34"], (tc + 1) * 32)
                    v.tensor_scalar_mul(out=m_sb, in0=ps_s3[:], scalar1=1.0 / 4096)
                    v.tensor_scalar_mul(out=s2m_sb, in0=ps_s4[:], scalar1=1.0 / 4096)
                    v.tensor_tensor(out=var_sb, in0=m_sb, in1=m_sb, op=ALU.mult)
                    v.tensor_tensor(out=var_sb, in0=s2m_sb, in1=var_sb,
                                    op=ALU.subtract).then_inc(sem["ST2V"], 1)
                    for j in range(NJT):
                        JD = tc * 32 + j
                        v.wait_ge(sem["M2BC"], tc + 1)
                        v.wait_ge(sem["R2L"], (JD + 1) * 16)
                        if JD >= 2:
                            v.wait_ge(sem["OUT"], (JD - 1) * 16)
                        v.tensor_tensor(out=t2_sb[:], in0=rl_sb[j % 2][:],
                                        in1=ps_m2[:], op=ALU.subtract)
                        v.tensor_tensor(out=o32_sb[j % 2][:], in0=t2_sb[:],
                                        in1=ps_r2[:], op=ALU.mult).then_inc(sem["OV"], 1)

                @block.scalar
                def _(s, tc=tc):
                    if tc == 0:
                        s.wait_ge(sem["ZSC"], 64 * 16)
                    for j in range(3):
                        JD = tc * 32 + j
                        if JD >= 4:
                            s.wait_ge(sem["R2V"], JD - 3)
                        s.dma_start(out=zj_sb[j % 4][:],
                                    in_=zsc[j * 128:(j + 1) * 128, tc * T:(tc + 1) * T]
                                    ).then_inc(sem["ZJL"], 16)
                    for j in range(NJT):
                        JD = tc * 32 + j
                        if j + 3 < NJT:
                            if JD >= 1:
                                s.wait_ge(sem["R2V"], JD)
                            s.dma_start(out=zj_sb[(j + 3) % 4][:],
                                        in_=zsc[(j + 3) * 128:(j + 4) * 128, tc * T:(tc + 1) * T]
                                        ).then_inc(sem["ZJL"], 16)
                        s.wait_ge(sem["R2V"], JD + 1)
                        if JD >= 2:
                            s.wait_ge(sem["S34"], JD - 1)
                        s.activation(out=sq2_sb[j % 2][:], in_=r2_sb[j % 2][:],
                                     func=AF.Square).then_inc(sem["SQ2"], 1)
                    s.wait_ge(sem["ST2V"], tc + 1)
                    s.activation(out=var_sb, in_=var_sb, func=AF.Ln,
                                 bias=cst_sb[0:1, 1:2])
                    s.activation(out=rstd_sb, in_=var_sb, func=AF.Exp,
                                 scale=cst_sb[0:1, 2:3]).then_inc(sem["ST2S"], 1)

        @block.gpsimd
        def _(g):
            g.wait_ge(sem["OUT"], 64 * 16)

    return nc


_NC_CACHE = None


def kernel(L0_attn_x, L0_attn_A_log, L0_attn_D, L0_attn_x_proj_w,
           L0_attn_dt_proj_w, L0_attn_dt_proj_b, L0_ffn_w_gate,
           L0_ffn_w_up, L0_ffn_w_down):
    global _NC_CACHE
    x = np.asarray(L0_attn_x, np.float32)
    A_log = np.asarray(L0_attn_A_log, np.float32)
    Dv = np.asarray(L0_attn_D, np.float32)
    xpw = np.asarray(L0_attn_x_proj_w, np.float32)
    dtw = np.asarray(L0_attn_dt_proj_w, np.float32)
    dtb = np.asarray(L0_attn_dt_proj_b, np.float32)
    wgate = np.asarray(L0_ffn_w_gate, np.float32)
    wup = np.asarray(L0_ffn_w_up, np.float32)
    wdown = np.asarray(L0_ffn_w_down, np.float32)

    xfm = np.ascontiguousarray(x.transpose(0, 2, 1))
    xpwt = np.ascontiguousarray(
        xpw.T.reshape(32, 128, 160).transpose(1, 0, 2)).reshape(128, 5120)
    wg_t = np.ascontiguousarray(
        wgate.reshape(86, 128, 32, 128).transpose(0, 3, 2, 1)).reshape(86, 128, 4096)
    wu_t = np.ascontiguousarray(
        wup.reshape(86, 128, 32, 128).transpose(0, 3, 2, 1)).reshape(86, 128, 4096)
    wd_t = np.ascontiguousarray(
        wdown.reshape(32, 128, 86, 128).transpose(0, 3, 2, 1)).reshape(32, 128, 11008)
    ones1 = np.ones((128, 1), np.float32)
    onebc = np.ones((1, 128), np.float32)

    in_maps = []
    for c in range(8):
        b, half = c // 2, c % 2
        d0 = half * 2048
        in_maps.append(dict(
            xfm=xfm[b],
            xhh=np.ascontiguousarray(xfm[b, d0:d0 + 2048, :]),
            xpwt=xpwt,
            dtwt=np.ascontiguousarray(dtw[d0:d0 + 2048].T),
            dtbi=np.ascontiguousarray(dtb[d0:d0 + 2048].reshape(16, 128).T),
            alog=np.ascontiguousarray(
                A_log[d0:d0 + 2048].reshape(16, 128, 16).transpose(1, 0, 2)).reshape(128, 256),
            dvec=np.ascontiguousarray(Dv[d0:d0 + 2048].reshape(16, 128).T),
            ones1=ones1, onebc=onebc,
            wg=wg_t, wu=wu_t, wd=wd_t,
            toko=np.array([[half * 1024]], np.int32),
        ))

    if _NC_CACHE is None:
        _NC_CACHE = build()
    res = run_bass_kernel_spmd(_NC_CACHE, in_maps, core_ids=list(range(8)))

    out = np.empty((4, 2048, 4096), np.float32)
    for c in range(8):
        b, half = c // 2, c % 2
        out[b, half * 1024:(half + 1) * 1024, :] = res.results[c]["outfm"].T
    return out
